# version 13
# baseline (speedup 1.0000x reference)
import numpy as np
import ml_dtypes
from contextlib import ExitStack

import concourse.bass as bass
import concourse.mybir as mybir
from concourse.bass_utils import run_bass_kernel_spmd
from concourse.replica_groups import maybe_share_collective_output_space

F32 = mybir.dt.float32
BF16 = mybir.dt.bfloat16
U8 = mybir.dt.uint8
AF = mybir.ActivationFunctionType
ALU = mybir.AluOpType
AX = mybir.AxisListType

BFnp = ml_dtypes.bfloat16

R = 1024
DM = 2048
HID = 5632
T = 128
NCH = 8
HALF = 32
NK = 16
NHT = 44
NG = 22
NDQ = 8
RG = [[0, 1, 2, 3, 4, 5, 6, 7]]
NGRP = 2 * NHT

RO_SCR1 = 0
RO_SCR2 = 32
RO_SP = 64
RO_LAMRE = 96
RO_DELTA = 128
RO_AA = 160
RO_BB = 192
RO_NAA = 224
RO_B8 = 256
RO_B16 = 288
RO_S = 320
RO_C = 352
RO_TT1 = 384
RO_TT2 = 416
RO_EA = 448
RO_EAI = 480
RO_LAM1 = 512
RO_EXT = 576
RO_LAM128 = 1088
RO_LAM1024 = 1152
RO_SQA = 1216
RO_SQB = 1280
RO_T1 = 1344
RO_T2 = 1376
RO_GC = 256
RO_GATH = 0
RW_LAMPOW = 0
RW_K = 512
RW_ACC = 576


def build_nc():
    nc = bass.Bass("TRN2", target_bir_lowering=False,
                   detect_race_conditions=False)

    u_d = nc.dram_tensor("u", [R, DM], F32, kind="ExternalInput").ap()
    wgt_d = nc.dram_tensor("wgt", [DM, HID], U8, kind="ExternalInput").ap()
    wut_d = nc.dram_tensor("wut", [DM, HID], U8, kind="ExternalInput").ap()
    wdt_d = nc.dram_tensor("wdt", [HID, DM], U8, kind="ExternalInput").ap()
    sgsu_d = nc.dram_tensor("sgsu", [128, 2, NHT], F32, kind="ExternalInput").ap()
    sdrow_d = nc.dram_tensor("sdrow", [1, DM], F32, kind="ExternalInput").ap()
    drow_d = nc.dram_tensor("drow", [1, DM], F32, kind="ExternalInput").ap()
    bret_d = nc.dram_tensor("bret", [DM, HALF], F32, kind="ExternalInput").ap()
    bimt_d = nc.dram_tensor("bimt", [DM, HALF], F32, kind="ExternalInput").ap()
    crt_d = nc.dram_tensor("crt", [HALF, DM], F32, kind="ExternalInput").ap()
    cit_d = nc.dram_tensor("cit", [HALF, DM], F32, kind="ExternalInput").ap()
    prow_d = nc.dram_tensor("prow", [1, 128], F32, kind="ExternalInput").ap()
    identbf_d = nc.dram_tensor("identbf", [128, 128], BF16, kind="ExternalInput").ap()
    identf_d = nc.dram_tensor("identf", [HALF, HALF], F32, kind="ExternalInput").ap()
    utri_d = nc.dram_tensor("utri", [128, 128], F32, kind="ExternalInput").ap()
    e127_d = nc.dram_tensor("e127", [128, 1], F32, kind="ExternalInput").ap()
    onesr_d = nc.dram_tensor("onesr", [1, 128], F32, kind="ExternalInput").ap()
    onesrb_d = nc.dram_tensor("onesrb", [1, 128], BF16, kind="ExternalInput").ap()
    m128_d = nc.dram_tensor("m128", [128, 1], F32, kind="ExternalInput").ap()
    onesone_d = nc.dram_tensor("onesone", [1, 1], F32, kind="ExternalInput").ap()
    msel_d = nc.dram_tensor("msel", [128, 8], F32, kind="ExternalInput").ap()
    gsel_d = nc.dram_tensor("gsel", [1, 4], F32, kind="ExternalInput").ap()
    out_d = nc.dram_tensor("out", [R, DM], F32, kind="ExternalOutput").ap()

    aspace = maybe_share_collective_output_space("AllGather", RG, prefer_local=True)
    gcin = nc.dram_tensor("gcin", [16, 128], F32)
    gcout = nc.dram_tensor("gcout", [128, 128], F32, addr_space=aspace)

    ctx = ExitStack()
    sb = lambda n, sh, dt: ctx.enter_context(nc.sbuf_tensor(n, sh, dt))
    sem = lambda n: ctx.enter_context(nc.semaphore(n))

    xnT = sb("xnT", [128, NK, R], BF16)
    gate = sb("gate", [128, NHT, R], BF16)
    wstg = sb("wstg", [128, 2, HID], BF16)
    urng = sb("urng", [128, 2, DM], F32)
    xsn = sb("xsn", [128, 512], F32)
    xsb = sb("xsb", [128, 512], BF16)
    qp8p = sb("qp8p", [128, 512], F32)
    xsT = sb("xsT", [64, R], BF16)
    mbuf = sb("mbuf", [128, 2, 256], F32)
    rows = sb("rows", [1, 1536], F32)
    rows2 = sb("rows2", [1, 640], F32)
    bbarT = sb("bbarT", [128, NK, 64], BF16)
    cmat = sb("cmat", [64, DM], BF16)
    sgsu = sb("sgsu_sb", [128, 2, NHT], F32)
    identbf = sb("identbf_sb", [128, 128], BF16)
    identf = sb("identf_sb", [HALF, HALF], F32)
    utri = sb("utri_sb", [128, 128], F32)
    e127 = sb("e127_sb", [128, 1], F32)
    onesr = sb("onesr_sb", [1, 128], F32)
    onesrb = sb("onesrb_sb", [1, 128], BF16)
    m128c = sb("m128c_sb", [128, 1], F32)
    onesone = sb("onesone_sb", [1, 1], F32)
    msel = sb("msel_sb", [128, 8], F32)
    gsel = sb("gsel_sb", [1, 4], F32)
    prow = sb("prow_sb", [1, 128], F32)
    s1c = sb("s1c", [128, 2], F32)
    s2c = sb("s2c", [128, 2], F32)
    meanc = sb("meanc", [128, 2], F32)
    varc = sb("varc", [128, 2], F32)
    rstdc = sb("rstdc", [128, 2], F32)
    negmrc = sb("negmrc", [128, 2], F32)
    scrc = sb("scrc", [128, 2], F32)
    epsc = sb("epsc", [128, 1], F32)
    seedc = sb("seedc", [HALF, 8], F32)
    gath = sb("gath", [128, 128], F32)
    rowbf = sb("rowbf", [1, 8, 512], BF16)
    ctxAB = ExitStack()
    sbAB = lambda n, sh, dt: ctxAB.enter_context(nc.sbuf_tensor(n, sh, dt))
    xn_tm = sbAB("xn_tm", [128, 2, DM], BF16)
    pqcm = sbAB("pqcm", [HALF, 2, 256], F32)
    zz = sbAB("zz", [128, 512], F32)
    qpq = sbAB("qpq", [128, 64], F32)
    tmpB2 = sbAB("tmpB2", [HALF, 4, 128], F32)
    mcol = sbAB("mcol", [HALF, 2], F32)
    tdbl = sbAB("tdbl", [HALF, 2, 64], F32)

    sAUX = sem("sAUX"); sU = sem("sU"); sSQ = sem("sSQ"); sVE1 = sem("sVE1")
    sAC1 = sem("sAC1"); sRSTD = sem("sRSTD"); sLN = sem("sLN"); sTPp = sem("sTPp")
    sTEU = sem("sTEU"); sPV = sem("sPV"); sPA = sem("sPA")
    sSEEDm = sem("sSEEDm"); sSEEDe = sem("sSEEDe"); sPQ = sem("sPQ")
    sTEB = sem("sTEB"); sBBD = sem("sBBD"); sBB = sem("sBB"); sMMB = sem("sMMB")
    sZ = sem("sZ"); sMMC = sem("sMMC"); sXN = sem("sXN"); sEXT = sem("sEXT")
    sGC = sem("sGC"); sCC = sem("sCC"); sGATH = sem("sGATH")
    sW = sem("sW"); sCV = sem("sCV"); sMG = sem("sMG")
    sEG = sem("sEG"); sCM = sem("sCM")
    sSEL = sem("sSEL"); sKC = sem("sKC"); sFIX = sem("sFIX"); sXSB = sem("sXSB")
    sMXT = sem("sMXT"); sXT = sem("sXT")
    sROWB = sem("sROWB"); sROWC = sem("sROWC")
    sD1E = sem("sD1E"); sSDE = sem("sSDE")
    sW2 = sem("sW2"); sCV2 = sem("sCV2")
    sWDd = sem("sWDd"); sCCD = sem("sCCD"); sVS = sem("sVS")
    sUS = sem("sUS"); sEVD = sem("sEVD"); sOUT = sem("sOUT")
    sDNE = sem("sDNE"); sCRm = sem("sCRm"); sBBR = sem("sBBR")
    vsn = [0]

    def mkser(v):
        class _W:
            def __getattr__(self, name):
                fn = getattr(v, name)

                def call(*a, **k):
                    v.wait_ge(sVS, vsn[0])
                    ins = fn(*a, **k)
                    ins.then_inc(sVS, 1)
                    vsn[0] += 1
                    return ins
                return call
        return _W()

    def vsig(v, sm, n=1):
        v.wait_ge(sVS, vsn[0])
        v.engine_nop().then_inc(sm, n)

    def r_(off, n=64):
        return rows[0:1, off:off + n]

    def r2_(off, n=64):
        return rows2[0:1, off:off + n]

    NAUX = 12

    resctx = ExitStack()
    pF32a = resctx.enter_context(nc.psum_tensor("pF32a", [128, 512], F32))
    pF32b = resctx.enter_context(nc.psum_tensor("pF32b", [128, 512], F32))
    pExt = resctx.enter_context(nc.psum_tensor("pExt", [1, 512], F32))

    psA = ExitStack()

    with nc.Block() as blk:

        @blk.sync
        def _(s):
            auxes = [(identbf, identbf_d), (identf, identf_d), (utri, utri_d),
                     (e127, e127_d), (onesr, onesr_d), (onesrb, onesrb_d),
                     (m128c, m128_d), (onesone, onesone_d), (msel, msel_d),
                     (gsel, gsel_d), (prow, prow_d), (sgsu, sgsu_d)]
            assert len(auxes) == NAUX
            for dst, src in auxes:
                idx = tuple(slice(None) for _ in dst.shape)
                s.dma_start(out=dst[idx], in_=src).then_inc(sAUX, 16)
            for i in range(8):
                if i >= 2:
                    s.wait_ge(sLN, i - 1)
                s.dma_start(out=urng[:, i % 2, :],
                            in_=u_d[128 * i:128 * (i + 1), :]).then_inc(sU, 16)
                if i >= 2:
                    s.dma_start(out=xnT[:, :, 128 * (i - 2):128 * (i - 1)],
                                in_=xn_tm[:, (i - 2) % 2, :],
                                transpose=True).then_inc(sTPp, 16)
            for i in (6, 7):
                s.wait_ge(sLN, i + 1)
                s.dma_start(out=xnT[:, :, 128 * i:128 * (i + 1)],
                            in_=xn_tm[:, i % 2, :],
                            transpose=True).then_inc(sTPp, 16)
            s.wait_ge(sTPp, 128)

        @blk.gpsimd
        def _(g):
            for S in range(2):
                src = wgt_d[:, 256 * S:256 * (S + 1)].rearrange("(k p) j -> p k j", p=128)
                dst = wstg[:, S % 2, 0:4096].rearrange("p (k j) -> p k j", k=16)
                g.dma_start(out=dst, in_=src).then_inc(sW, 16)

        @blk.vector
        def _(v):
            w = mkser(v)
            v.wait_ge(sAUX, 16 * NAUX)
            Lu = prow[0:1, 0:32]; Lim = prow[0:1, 32:64]
            w.tensor_scalar_mul(r_(RO_TT1, 32), Lu, -1.0)
            w.tensor_max(r_(RO_SCR1, 32), r_(RO_TT1, 32), Lu)
            w.tensor_scalar_mul(r_(RO_SCR1, 32), r_(RO_SCR1, 32), -1.0)
            vsig(v, sPV)
            v.wait_ge(sPA, 1)
            w.tensor_scalar_add(r_(RO_SCR2, 32), r_(RO_SCR2, 32), 1.0)
            vsig(v, sPV)
            v.wait_ge(sPA, 2)
            w.tensor_scalar_max(r_(RO_SP, 32), Lu, 0.0)
            w.tensor_add(r_(RO_SP, 32), r_(RO_SP, 32), r_(RO_SCR2, 32))
            w.tensor_scalar_add(r_(RO_LAMRE, 32), r_(RO_SP, 32), 0.0101)
            w.tensor_scalar_mul(r_(RO_LAMRE, 32), r_(RO_LAMRE, 32), -1.0)
            vsig(v, sPV)
            v.wait_ge(sPA, 3)
            w.tensor_mul(r_(RO_AA, 32), r_(RO_LAMRE, 32), r_(RO_DELTA, 32))
            w.tensor_mul(r_(RO_BB, 32), Lim, r_(RO_DELTA, 32))
            w.tensor_scalar_mul(r_(RO_NAA, 32), r_(RO_AA, 32), -1.0)
            w.tensor_scalar_mul(r_(RO_B8, 32), r_(RO_BB, 32), 0.125)
            w.tensor_scalar_mul(r_(RO_B16, 32), r_(RO_BB, 32), 0.0625)
            vsig(v, sPV)
            v.wait_ge(sPA, 4)
            w.tensor_mul(r_(RO_TT1, 32), r_(RO_B16, 32), r_(RO_B16, 32))
            w.tensor_scalar_mul(r_(RO_TT1, 32), r_(RO_TT1, 32), -2.0)
            w.tensor_scalar_add(r_(RO_C, 32), r_(RO_TT1, 32), 1.0)
            for _i in range(3):
                w.tensor_mul(r_(RO_TT1, 32), r_(RO_S, 32), r_(RO_C, 32))
                w.tensor_mul(r_(RO_TT2, 32), r_(RO_C, 32), r_(RO_C, 32))
                w.tensor_mul(r_(RO_S, 32), r_(RO_S, 32), r_(RO_S, 32))
                w.tensor_sub(r_(RO_C, 32), r_(RO_TT2, 32), r_(RO_S, 32))
                w.tensor_scalar_mul(r_(RO_S, 32), r_(RO_TT1, 32), 2.0)
            w.tensor_mul(r_(RO_LAM1, 32), r_(RO_EA, 32), r_(RO_C, 32))
            w.tensor_mul(rows[0:1, RO_LAM1 + 32:RO_LAM1 + 64], r_(RO_EA, 32), r_(RO_S, 32))
            w.tensor_mul(r_(RO_SQA, 32), r_(RO_EAI, 32), r_(RO_C, 32))
            w.tensor_mul(rows[0:1, RO_SQA + 32:RO_SQA + 64], r_(RO_EAI, 32), r_(RO_S, 32))
            w.tensor_scalar_mul(rows[0:1, RO_SQA + 32:RO_SQA + 64],
                                rows[0:1, RO_SQA + 32:RO_SQA + 64], -1.0)
            w.tensor_mul(r_(RO_SP, 32), r_(RO_LAMRE, 32), r_(RO_LAMRE, 32))
            w.tensor_mul(r_(RO_TT1, 32), Lim, Lim)
            w.tensor_add(r_(RO_SP, 32), r_(RO_SP, 32), r_(RO_TT1, 32))
            w.reciprocal(r_(RO_DELTA, 32), r_(RO_SP, 32))
            w.tensor_scalar_add(r_(RO_AA, 32), r_(RO_LAM1, 32), -1.0)
            w.tensor_mul(r_(RO_TT1, 32), r_(RO_AA, 32), r_(RO_LAMRE, 32))
            w.tensor_mul(r_(RO_TT2, 32), rows[0:1, RO_LAM1 + 32:RO_LAM1 + 64], Lim)
            w.tensor_add(r_(RO_TT1, 32), r_(RO_TT1, 32), r_(RO_TT2, 32))
            w.tensor_mul(r_(RO_SCR1, 32), r_(RO_TT1, 32), r_(RO_DELTA, 32))
            w.tensor_mul(r_(RO_TT1, 32), rows[0:1, RO_LAM1 + 32:RO_LAM1 + 64], r_(RO_LAMRE, 32))
            w.tensor_mul(r_(RO_TT2, 32), r_(RO_AA, 32), Lim)
            w.tensor_sub(r_(RO_TT1, 32), r_(RO_TT1, 32), r_(RO_TT2, 32))
            w.tensor_mul(r_(RO_SCR2, 32), r_(RO_TT1, 32), r_(RO_DELTA, 32))
            vsig(v, sPV)

            w.memset(epsc[:, 0:1], 0.0)
            w.tensor_scalar_add(epsc[:, 0:1], epsc[:, 0:1], 1e-5)
            for i in range(9):
                if i < 8:
                    sl = i % 2
                    v.wait_ge(sU, 16 * (i + 1))
                    w.reduce_sum(s1c[:, sl:sl + 1], urng[:, sl, :], axis=AX.X)
                    v.wait_ge(sSQ, i + 1)
                    w.tensor_scalar_mul(meanc[:, sl:sl + 1], s1c[:, sl:sl + 1], 1.0 / DM)
                    w.tensor_mul(scrc[:, sl:sl + 1], meanc[:, sl:sl + 1], meanc[:, sl:sl + 1])
                    w.scalar_tensor_tensor(varc[:, sl:sl + 1], s2c[:, sl:sl + 1], 1.0 / DM,
                                           scrc[:, sl:sl + 1], ALU.mult, ALU.subtract)
                    vsig(v, sVE1)
                if i < 8:
                    sl = i % 2
                    v.wait_ge(sAC1, i + 1)
                    w.reciprocal(rstdc[:, sl:sl + 1], scrc[:, sl:sl + 1])
                    w.scalar_tensor_tensor(negmrc[:, sl:sl + 1], meanc[:, sl:sl + 1], -1.0,
                                           rstdc[:, sl:sl + 1], ALU.mult, ALU.mult)
                    vsig(v, sRSTD)
            v.wait_ge(sSEEDm, 1)
            w.tensor_copy(seedc[:, 0:6], pF32a[0:HALF, 0:6])
            vsig(v, sSEEDe)

        @blk.scalar
        def _(a):
            a.wait_ge(sAUX, 16 * NAUX)
            a.wait_ge(sPV, 1)
            a.activation(r_(RO_SCR2, 32), r_(RO_SCR1, 32), AF.Exp).then_inc(sPA, 1)
            a.wait_ge(sPV, 2)
            a.activation(r_(RO_SCR2, 32), r_(RO_SCR2, 32), AF.Ln).then_inc(sPA, 1)
            a.wait_ge(sPV, 3)
            a.activation(r_(RO_DELTA, 32), prow[0:1, 64:96], AF.Exp).then_inc(sPA, 1)
            a.wait_ge(sPV, 4)
            a.activation(r_(RO_EA, 32), r_(RO_AA, 32), AF.Exp)
            a.activation(r_(RO_EAI, 32), r_(RO_NAA, 32), AF.Exp)
            a.activation(r_(RO_S, 32), r_(RO_B8, 32), AF.Sin)
            a.activation(r_(RO_B16, 32), r_(RO_B16, 32), AF.Sin).then_inc(sPA, 1)
            a.wait_ge(sU, 16)
            a.activation(gate[:, 0:2, :].rearrange("p a b -> p (a b)"),
                         urng[:, 0, :], AF.Square,
                         accum_out=s2c[:, 0:1]).then_inc(sSQ, 1)
            for i in range(9):
                if i + 1 < 8:
                    sl2 = (i + 1) % 2
                    a.wait_ge(sU, 16 * (i + 2))
                    if i >= 1:
                        a.wait_ge(sVE1, i)
                    a.activation(gate[:, 0:2, :].rearrange("p a b -> p (a b)"),
                                 urng[:, sl2, :], AF.Square,
                                 accum_out=s2c[:, sl2:sl2 + 1]).then_inc(sSQ, 1)
                if i < 8:
                    sl = i % 2
                    a.wait_ge(sVE1, i + 1)
                    a.activation(scrc[:, sl:sl + 1], varc[:, sl:sl + 1],
                                 AF.Sqrt, bias=epsc[:, 0:1]).then_inc(sAC1, 1)
                    a.wait_ge(sRSTD, i + 1)
                    if i >= 2:
                        a.wait_ge(sTPp, 16 * (i - 1))
                    a.activation(xn_tm[:, sl, :], urng[:, sl, :], AF.Identity,
                                 bias=negmrc[:, sl:sl + 1],
                                 scale=rstdc[:, sl:sl + 1]).then_inc(sLN, 1)

        @blk.tensor
        def _(t):
            t.wait_ge(sAUX, 16 * NAUX)
            t.wait_ge(sPV, 5)
            seeds = [RO_LAM1, RO_LAM1 + 32, RO_SQA, RO_SQA + 32, RO_SCR1, RO_SCR2]
            ins = None
            for j, off in enumerate(seeds):
                ins = t.matmul(pF32a[0:HALF, j:j + 1], rows[0:1, off:off + 32],
                               onesone[0:1, 0:1], start=True, stop=True)
            ins.then_inc(sSEEDm, 1)

    psA.close()

    psB = ExitStack()
    pGa = psB.enter_context(nc.psum_tensor("pGa", [128, 1024], F32))
    pGb = psB.enter_context(nc.psum_tensor("pGb", [128, 1024], F32))
    pAux = psB.enter_context(nc.psum_tensor("pAux", [128, 512], F32))
    pG = [pGa, pGb]

    def stripe_dma(g, S):
        wsrc = wgt_d if S < NG else wut_d
        gg = S % NG
        src = wsrc[:, 256 * gg:256 * (gg + 1)].rearrange("(k p) j -> p k j", p=128)
        dst = wstg[:, S % 2, 0:4096].rearrange("p (k j) -> p k j", k=16)
        if S >= 2:
            g.wait_ge(sMG, 2 * S - 2)
        g.dma_start(out=dst, in_=src).then_inc(sW, 16)

    def conv_stripe(a, S):
        a.wait_ge(sW, 16 * (S + 1))
        a.activation(wstg[:, S % 2, 0:4096], wstg[:, S % 2, 0:4096], AF.Identity,
                     bias=m128c[:, 0:1]).then_inc(sCV, 1)

    def mm_group1(t, G):
        S = G // 2
        hh = G % 2
        t.wait_ge(sCV, S + 1)
        if G >= 2:
            t.wait_ge(sEG, G - 1)
        ins = None
        for k in range(NK):
            lhs = wstg[:, S % 2, 256 * k + 128 * hh:256 * k + 128 * (hh + 1)]
            for th in range(2):
                ins = t.matmul(pG[G % 2][:, 512 * th:512 * (th + 1)], lhs,
                               xnT[:, k, 512 * th:512 * (th + 1)],
                               start=(k == 0), stop=(k == NK - 1))
        ins.then_inc(sMG, 1)

    with nc.Block() as blk:

        @blk.gpsimd
        def _(g):
            for q in range(8):
                srcd = drow_d if q < 4 else sdrow_d
                qq = q % 4
                g.dma_start(out=rowbf[0:1, q, :],
                            in_=srcd[0:1, 512 * qq:512 * (qq + 1)]).then_inc(sROWB, 16)
            for S in range(2, 22):
                stripe_dma(g, S)
            g.wait_ge(sCC, 256)
            g.collective_compute(
                "AllGather", ALU.bypass, replica_groups=RG,
                ins=[gcin.ap().opt()], outs=[gcout.ap().opt()],
            ).then_inc(sCCD)
            for S in range(22, 2 * NG):
                stripe_dma(g, S)
            for Dh in range(2):
                dq, q = Dh // 4, Dh % 4
                g.wait_ge(sMG, 86 + 2 * Dh)
                src = wdt_d[1408 * q:1408 * (q + 1),
                            512 * dq:512 * (dq + 1)].rearrange("(k p) j -> p k j", p=128)
                dst = wstg[:, Dh % 2, 0:5632].rearrange("p (k j) -> p k j", k=11)
                g.dma_start(out=dst, in_=src).then_inc(sW2, 16)

        @blk.sync
        def _(s):
            s.dma_start(out=xsn[:, :].rearrange("p (k c) -> p k c", c=HALF),
                        in_=bret_d.rearrange("(k p) c -> p k c", p=128)).then_inc(sBBD, 16)
            s.dma_start(out=mbuf[:, :, :].rearrange("p a (q c) -> p (a q) c", c=HALF),
                        in_=bimt_d.rearrange("(k p) c -> p k c", p=128)).then_inc(sBBD, 16)
            for n in range(16):
                cpart, cch = n // 8, n % 8
                srcd = crt_d if cpart == 0 else cit_d
                if n >= 1:
                    s.wait_ge(sCM, n)
                s.dma_start(out=tmpB2[:, 0:2, :].rearrange("p a b -> p (a b)"),
                            in_=srcd[:, 256 * cch:256 * (cch + 1)]).then_inc(sBBD, 16)
            s.wait_ge(sGC, 1)
            for rr in range(8):
                s.dma_start(out=gcin.ap()[2 * rr:2 * rr + 1, :],
                            in_=rows[0:1, RO_GC:RO_GC + 128]).then_inc(sCC, 16)
                s.dma_start(out=gcin.ap()[2 * rr + 1:2 * rr + 2, :],
                            in_=rows[0:1, RO_GC + 128:RO_GC + 256]).then_inc(sCC, 16)
            s.wait_ge(sCCD, 1)
            s.dma_start(out=gath[:, :], in_=gcout.ap()).then_inc(sGATH, 16)

        @blk.vector
        def _(v):
            w = mkser(v)

            def csq(dsto, srco):
                w.tensor_mul(r_(RO_T1, 32), rows[0:1, srco:srco + 32], rows[0:1, srco:srco + 32])
                w.tensor_mul(r_(RO_T2, 32), rows[0:1, srco + 32:srco + 64],
                             rows[0:1, srco + 32:srco + 64])
                w.tensor_mul(rows[0:1, dsto + 32:dsto + 64], rows[0:1, srco:srco + 32],
                             rows[0:1, srco + 32:srco + 64])
                w.tensor_scalar_mul(rows[0:1, dsto + 32:dsto + 64],
                                    rows[0:1, dsto + 32:dsto + 64], 2.0)
                w.tensor_sub(rows[0:1, dsto:dsto + 32], r_(RO_T1, 32), r_(RO_T2, 32))

            def wcmul(do, dT, ao, aT, bo, bT):
                t1 = r_(RO_T1, 32); t2 = r_(RO_T2, 32)
                w.tensor_mul(t1, aT(ao, 32), bT(bo, 32))
                w.tensor_mul(t2, aT(ao + 32, 32), bT(bo + 32, 32))
                w.tensor_sub(dT(do, 32), t1, t2)
                w.tensor_mul(t1, aT(ao, 32), bT(bo + 32, 32))
                w.tensor_mul(t2, aT(ao + 32, 32), bT(bo, 32))
                return w.tensor_add(dT(do + 32, 32), t1, t2)

            cur = RO_LAM1
            for i in range(7):
                dst = RO_SQA if i % 2 == 0 else RO_SQB
                if i == 6:
                    dst = RO_LAM128
                csq(dst, cur)
                cur = dst
            cur = RO_LAM128
            for i in range(3):
                dst = RO_SQA if i % 2 == 0 else RO_SQB
                if i == 2:
                    dst = RO_LAM1024
                csq(dst, cur)
                cur = dst
            w.memset(r2_(RW_LAMPOW), 0.0)
            w.tensor_scalar_add(r2_(RW_LAMPOW, 32), r2_(RW_LAMPOW, 32), 1.0)
            for chp in range(7):
                wcmul(RW_LAMPOW + 64 * (chp + 1), r2_, RW_LAMPOW + 64 * chp, r2_,
                      RO_LAM128, r_)

            for sidx, soff in ((0, 2), (1, 0)):
                re = lambda a, b: pqcm[0:HALF, sidx, a:a + b]
                im = lambda a, b: pqcm[0:HALF, sidx, 128 + a:128 + a + b]
                sre = seedc[:, soff:soff + 1]
                sim = seedc[:, soff + 1:soff + 2]
                w.memset(pqcm[0:HALF, sidx, :], 0.0)
                w.tensor_scalar_add(re(0, 1), re(0, 1), 1.0)
                w.tensor_copy(re(1, 1), sre)
                w.tensor_copy(im(1, 1), sim)
                n = 2
                while n < T:
                    m = min(n, T - n)
                    w.tensor_mul(mcol[:, 0:1], re(n - 1, 1), sre)
                    w.tensor_mul(tdbl[:, 0, 0:1], im(n - 1, 1), sim)
                    w.tensor_sub(mcol[:, 0:1], mcol[:, 0:1], tdbl[:, 0, 0:1])
                    w.tensor_mul(mcol[:, 1:2], re(n - 1, 1), sim)
                    w.tensor_mul(tdbl[:, 0, 0:1], im(n - 1, 1), sre)
                    w.tensor_add(mcol[:, 1:2], mcol[:, 1:2], tdbl[:, 0, 0:1])
                    w.tensor_scalar_mul(tdbl[:, 0, 0:m], re(0, m), mcol[:, 0:1])
                    w.tensor_scalar_mul(tdbl[:, 1, 0:m], im(0, m), mcol[:, 1:2])
                    w.tensor_sub(re(n, m), tdbl[:, 0, 0:m], tdbl[:, 1, 0:m])
                    w.tensor_scalar_mul(tdbl[:, 0, 0:m], re(0, m), mcol[:, 1:2])
                    w.tensor_scalar_mul(tdbl[:, 1, 0:m], im(0, m), mcol[:, 0:1])
                    w.tensor_add(im(n, m), tdbl[:, 0, 0:m], tdbl[:, 1, 0:m])
                    n += n
            vsig(v, sPQ)

            cre = urng[:, 0, 0:512]
            cim = urng[:, 1, 0:512]
            mbf = mbuf[:, :, :].rearrange("p a b -> p (a b)")
            v.wait_ge(sCRm, 1)
            w.tensor_copy(urng[:, 0, 0:32], pAux[:, 0:32])
            w.tensor_copy(urng[:, 1, 0:32], pAux[:, 32:64])
            for half in range(2):
                for wd in (32, 64, 128, 256):
                    w.tensor_copy(urng[:, half, wd:2 * wd], urng[:, half, 0:wd])
            v.wait_ge(sBBD, 32)
            w.tensor_mul(zz[:, :], xsn[:, :], cre)
            w.tensor_mul(qp8p[:, :], mbf, cim)
            w.tensor_sub(zz[:, :], zz[:, :], qp8p[:, :])
            w.tensor_mul(qp8p[:, :], xsn[:, :], cim)
            w.tensor_mul(xsn[:, :], mbf, cre)
            w.tensor_add(qp8p[:, :], qp8p[:, :], xsn[:, :])
            w.tensor_copy(bbarT[:, :, 0:32],
                          zz[:, :].rearrange("p (k c) -> p k c", c=32))
            w.tensor_copy(bbarT[:, :, 32:64],
                          qp8p[:, :].rearrange("p (k c) -> p k c", c=32))
            vsig(v, sBBR)

            for n in range(16):
                cpart, cch = n // 8, n % 8
                sgn = 2.0 if cpart == 0 else -2.0
                v.wait_ge(sBBD, 32 + 16 * (n + 1))
                w.tensor_scalar_mul(cmat[32 * cpart:32 * cpart + 32,
                                         256 * cch:256 * (cch + 1)],
                                    tmpB2[:, 0:2, :].rearrange("p a b -> p (a b)"), sgn)
                vsig(v, sCM)

            for j in range(4):
                v.wait_ge(sTPp, 128 + j + 1)
                src = pF32a[:, 0:32] if j % 2 == 0 else pF32b[:, 0:32]
                if j < 2:
                    w.tensor_copy(qpq[:, 32 * j:32 * j + 32], src)
                else:
                    w.tensor_copy(qp8p[:, 32 * (j - 2):32 * (j - 2) + 32], src)
                vsig(v, sTEB)
            for c in range(1, 8):
                w.tensor_copy(qp8p[:, 64 * c:64 * (c + 1)], qp8p[:, 0:64])

            v.wait_ge(sMMB, 8)
            zzv = zz[:, :].rearrange("p (c n) -> p c n", n=64)
            buv = pF32a[:, :].rearrange("p (c n) -> p c n", n=64)
            ztv = mbuf[:, 0, :].rearrange("p (c n) -> p c n", n=32)
            qre = qpq[:, 0:32]; qim = qpq[:, 32:64]
            for ch in range(NCH):
                w.tensor_mul(zzv[:, ch, 0:32], qre, buv[:, ch, 0:32])
                w.tensor_mul(ztv[:, ch, :], qim, buv[:, ch, 32:64])
                w.tensor_sub(zzv[:, ch, 0:32], zzv[:, ch, 0:32], ztv[:, ch, :])
                w.tensor_mul(zzv[:, ch, 32:64], qre, buv[:, ch, 32:64])
                w.tensor_mul(ztv[:, ch, :], qim, buv[:, ch, 0:32])
                w.tensor_add(zzv[:, ch, 32:64], zzv[:, ch, 32:64], ztv[:, ch, :])
            vsig(v, sZ)

            v.wait_ge(sMMC, 1)
            xsv = xsn[:, :].rearrange("p (c n) -> p c n", n=64)
            cuv = pF32b[:, :].rearrange("p (c n) -> p c n", n=64)
            p8 = qp8p[:, :].rearrange("p (c n) -> p c n", n=64)
            m0 = mbuf[:, 0, :].rearrange("p (c n) -> p c n", n=32)
            w.tensor_mul(xsv[:, :, 0:32], p8[:, :, 0:32], cuv[:, :, 0:32])
            w.tensor_mul(m0, p8[:, :, 32:64], cuv[:, :, 32:64])
            w.tensor_sub(xsv[:, :, 0:32], xsv[:, :, 0:32], m0)
            w.tensor_mul(xsv[:, :, 32:64], p8[:, :, 0:32], cuv[:, :, 32:64])
            w.tensor_mul(m0, p8[:, :, 32:64], cuv[:, :, 0:32])
            w.tensor_add(xsv[:, :, 32:64], xsv[:, :, 32:64], m0)
            vsig(v, sXN)

            v.wait_ge(sEXT, 1)
            w.tensor_copy(rows[0:1, RO_EXT:RO_EXT + 512], pExt[0:1, :])
            for ch in range(1, NCH):
                dsto = RO_EXT + 64 * ch
                srco = RO_EXT + 64 * (ch - 1)
                w.tensor_mul(r_(RO_T1, 32), r_(RO_LAM128, 32), rows[0:1, srco:srco + 32])
                w.tensor_mul(r_(RO_T2, 32), rows[0:1, RO_LAM128 + 32:RO_LAM128 + 64],
                             rows[0:1, srco + 32:srco + 64])
                w.tensor_sub(r_(RO_T1, 32), r_(RO_T1, 32), r_(RO_T2, 32))
                w.tensor_mul(r_(RO_T2, 32), r_(RO_LAM128, 32), rows[0:1, srco + 32:srco + 64])
                w.tensor_add(rows[0:1, dsto:dsto + 32], rows[0:1, dsto:dsto + 32], r_(RO_T1, 32))
                w.tensor_mul(r_(RO_T1, 32), rows[0:1, RO_LAM128 + 32:RO_LAM128 + 64],
                             rows[0:1, srco:srco + 32])
                w.tensor_add(r_(RO_T1, 32), r_(RO_T1, 32), r_(RO_T2, 32))
                w.tensor_add(rows[0:1, dsto + 32:dsto + 64], rows[0:1, dsto + 32:dsto + 64],
                             r_(RO_T1, 32))
            w.memset(rows[0:1, RO_GC:RO_GC + 256], 0.0)
            w.tensor_copy(rows[0:1, RO_GC:RO_GC + 64], r_(RO_LAM1024))
            w.tensor_copy(rows[0:1, RO_GC + 128:RO_GC + 192], r_(RO_EXT + 64 * 7))
            vsig(v, sGC)

            def wcmul2(do, dT, ao, aT, bo, bT):
                t1 = r_(RO_T1, 32); t2 = r_(RO_T2, 32)
                w.tensor_mul(t1, aT(ao, 32), bT(bo, 32))
                yield
                w.tensor_mul(t2, aT(ao + 32, 32), bT(bo + 32, 32))
                yield
                w.tensor_sub(dT(do, 32), t1, t2)
                yield
                w.tensor_mul(t1, aT(ao, 32), bT(bo + 32, 32))
                yield
                w.tensor_mul(t2, aT(ao + 32, 32), bT(bo, 32))
                yield
                w.tensor_add(dT(do + 32, 32), t1, t2)
                yield

            def carry_gen():
                v.wait_ge(sSEL, 1)
                w.tensor_copy(rows[0:1, RO_GATH:RO_GATH + 512].rearrange(
                    "o (j n) -> o j n", n=64), pAux[0:1, :].rearrange(
                    "o (j n) -> o j n", n=64))
                yield
                w.memset(r2_(RW_K), 0.0)
                yield
                w.memset(r2_(RW_ACC), 0.0)
                yield
                for j in range(4):
                    for half in range(2):
                        w.tensor_scalar_mul(r_(RO_T1, 32),
                                            rows2[0:1, RW_K + 32 * half:RW_K + 32 * half + 32],
                                            gsel[0:1, j:j + 1])
                        yield
                        w.tensor_add(rows2[0:1, RW_ACC + 32 * half:RW_ACC + 32 * half + 32],
                                     rows2[0:1, RW_ACC + 32 * half:RW_ACC + 32 * half + 32],
                                     r_(RO_T1, 32))
                        yield
                    if j < 3:
                        aoff = RO_GATH + 64 * j
                        soff = RO_GATH + 256 + 64 * j
                        w.tensor_mul(r_(RO_T1, 32), rows[0:1, aoff:aoff + 32],
                                     r2_(RW_K, 32))
                        yield
                        w.tensor_mul(r_(RO_T2, 32), rows[0:1, aoff + 32:aoff + 64],
                                     rows2[0:1, RW_K + 32:RW_K + 64])
                        yield
                        w.tensor_sub(r_(RO_T1, 32), r_(RO_T1, 32), r_(RO_T2, 32))
                        yield
                        w.tensor_mul(r_(RO_T2, 32), rows[0:1, aoff:aoff + 32],
                                     rows2[0:1, RW_K + 32:RW_K + 64])
                        yield
                        w.tensor_mul(rows2[0:1, RW_K + 32:RW_K + 64],
                                     rows[0:1, aoff + 32:aoff + 64], r2_(RW_K, 32))
                        yield
                        w.tensor_add(rows2[0:1, RW_K + 32:RW_K + 64],
                                     rows2[0:1, RW_K + 32:RW_K + 64], r_(RO_T2, 32))
                        yield
                        w.tensor_add(rows2[0:1, RW_K + 32:RW_K + 64],
                                     rows2[0:1, RW_K + 32:RW_K + 64],
                                     rows[0:1, soff + 32:soff + 64])
                        yield
                        w.tensor_add(r2_(RW_K, 32), r_(RO_T1, 32), rows[0:1, soff:soff + 32])
                        yield
                for ch in range(NCH - 1, 0, -1):
                    dsto = RO_EXT + 64 * ch
                    yield from wcmul2(RO_SQA, r_, RW_LAMPOW + 64 * ch, r2_, RW_ACC, r2_)
                    w.tensor_add(r_(RO_SQA, 32), r_(RO_SQA, 32),
                                 rows[0:1, dsto - 64:dsto - 32])
                    yield
                    w.tensor_add(rows[0:1, RO_SQA + 32:RO_SQA + 64],
                                 rows[0:1, RO_SQA + 32:RO_SQA + 64],
                                 rows[0:1, dsto - 32:dsto])
                    yield
                    yield from wcmul2(dsto, r_, RO_LAM1, r_, RO_SQA, r_)
                yield from wcmul2(RO_EXT, r_, RO_LAM1, r_, RW_ACC, r2_)
                vsig(v, sKC)
                yield
                v.wait_ge(sFIX, 1)
                xsv = xsn[:, :].rearrange("p (c n) -> p c n", n=64)
                xbv = xsb[:, :].rearrange("p (c n) -> p c n", n=64)
                fxv = pAux[:, :].rearrange("p (c n) -> p c n", n=64)
                p8b = qp8p[:, :].rearrange("p (c n) -> p c n", n=64)
                f0 = mbuf[:, 0, :].rearrange("p (c n) -> p c n", n=32)
                f1 = mbuf[:, 1, :].rearrange("p (c n) -> p c n", n=32)
                w.tensor_mul(f0, p8b[:, :, 0:32], fxv[:, :, 0:32])
                yield
                w.tensor_mul(f1, p8b[:, :, 32:64], fxv[:, :, 32:64])
                yield
                w.tensor_sub(f0, f0, f1)
                yield
                w.tensor_add(xbv[:, :, 0:32], xsv[:, :, 0:32], f0)
                yield
                w.tensor_mul(f0, p8b[:, :, 0:32], fxv[:, :, 32:64])
                yield
                w.tensor_mul(f1, p8b[:, :, 32:64], fxv[:, :, 0:32])
                yield
                w.tensor_add(f0, f0, f1)
                yield
                w.tensor_add(xbv[:, :, 32:64], xsv[:, :, 32:64], f0)
                vsig(v, sXSB)
                yield

            cg = carry_gen()
            for G in range(NGRP // 2, NGRP):
                ht = G - NGRP // 2
                v.wait_ge(sMG, G + 1)
                v.scalar_tensor_tensor(gate[:, ht, :], pG[G % 2][:, :],
                                       sgsu[:, 1, ht:ht + 1], gate[:, ht, :],
                                       ALU.mult, ALU.mult).then_inc(sEG, 1)
                if G >= NGRP // 2 + 14:
                    for _ in range(7):
                        next(cg, None)
            for _ in cg:
                pass

        @blk.scalar
        def _(a):
            conv_stripe(a, 0)
            conv_stripe(a, 1)
            for G in range(NGRP // 2):
                a.wait_ge(sMG, G + 1)
                a.activation(gate[:, G, :], pG[G % 2][:, :], AF.Silu,
                             scale=sgsu[:, 0, G:G + 1]).then_inc(sEG, 1)
                if G % 2 == 1 and G // 2 + 2 < 24:
                    conv_stripe(a, G // 2 + 2)
            for S in range(24, 2 * NG):
                conv_stripe(a, S)

        @blk.tensor
        def _(t):
            for G in range(NGRP):
                mm_group1(t, G)
                if G == 2:
                    t.matmul(pAux[:, 0:64], onesr[0:1, 0:128], rows[0:1, 0:64],
                             start=True, stop=True).then_inc(sCRm, 1)
                if G == 56:
                    t.wait_ge(sGATH, 16)
                    ins2 = None
                    for j in range(8):
                        ins2 = t.matmul(pAux[0:1, 64 * j:64 * (j + 1)],
                                        msel[:, j:j + 1], gath[:, 0:64],
                                        start=True, stop=True)
                    ins2.then_inc(sSEL, 1)
                if G == 80:
                    t.wait_ge(sKC, 1)
                    ins2 = None
                    for ch in range(NCH):
                        ins2 = t.matmul(pAux[:, 64 * ch:64 * (ch + 1)], onesr[0:1, :],
                                        rows[0:1, RO_EXT + 64 * ch:RO_EXT + 64 * (ch + 1)],
                                        start=True, stop=True)
                    ins2.then_inc(sFIX, 1)
                if G == 16:
                    t.wait_ge(sPQ, 1)
                    t.wait_ge(sSEEDe, 1)
                    for j in range(4):
                        sidx = 0 if j < 2 else 1
                        ri = j % 2
                        if j >= 2:
                            t.wait_ge(sTEB, j - 1)
                        t.transpose(pF32a[:, 0:32] if j % 2 == 0 else pF32b[:, 0:32],
                                    pqcm[0:HALF, sidx, 128 * ri:128 * ri + 128],
                                    identf[:, :]).then_inc(sTPp, 1)
                if G == 20:
                    t.wait_ge(sTEB, 4)
                    t.wait_ge(sBBR, 1)
                    for i in range(8):
                        ins = None
                        for k in range(NK):
                            ins = t.matmul(pF32a[:, 64 * i:64 * (i + 1)],
                                           xnT[:, k, 128 * i:128 * (i + 1)],
                                           bbarT[:, k, :], start=(k == 0),
                                           stop=(k == NK - 1))
                        ins.then_inc(sMMB, 1)
                if G == 24:
                    t.wait_ge(sZ, 1)
                    ins = None
                    for ch in range(NCH):
                        ins = t.matmul(pF32b[:, 64 * ch:64 * (ch + 1)], utri[:, :],
                                       zz[:, 64 * ch:64 * (ch + 1)], start=True, stop=True)
                    ins.then_inc(sMMC, 1)
                if G == 26:
                    t.wait_ge(sXN, 1)
                    t.matmul(pExt[0:1, :], e127[:, :], xsn[:, :],
                             start=True, stop=True).then_inc(sEXT, 1)

    psB.close()
    resctx.close()
    ctxAB.close()
    sdrep = sb("sdrep", [128, DM], BF16)
    drep = sb("drep", [128, DM], BF16)

    with nc.Block() as blk:
        ps3 = ExitStack()
        pXT = ps3.enter_context(nc.psum_tensor("pXT", [64, 8, 128], BF16))
        pBig = ps3.enter_context(nc.psum_tensor("pBig", [128, 2048], F32))

        @blk.gpsimd
        def _(g):
            for Dh in range(2, 16):
                g.wait_ge(sWDd, Dh - 1)
                dq, q = Dh // 4, Dh % 4
                srcw = wdt_d[1408 * q:1408 * (q + 1),
                             512 * dq:512 * (dq + 1)].rearrange(
                                 "(k p) j -> p k j", p=128)
                dstw = wstg[:, Dh % 2, 0:5632].rearrange("p (k j) -> p k j", k=11)
                g.dma_start(out=dstw, in_=srcw).then_inc(sW2, 16)

        @blk.tensor
        def _(t):
            for q in range(8):
                t.wait_ge(sROWB, 16 * (q + 1))
                if q == 4:
                    t.wait_ge(sD1E, 1)
                t.matmul(pBig[:, 512 * (q % 4):512 * (q % 4 + 1)], onesrb[0:1, :],
                         rowbf[0:1, q, :], start=True, stop=True).then_inc(sROWC, 1)
            t.wait_ge(sXSB, 1)
            ins = None
            for ch in range(NCH):
                ins = t.transpose(pXT[:, ch, :], xsb[:, 64 * ch:64 * (ch + 1)],
                                  identbf[:, :])
            ins.then_inc(sMXT, 1)

        def conv_down(v, Dh):
            dq = Dh // 4
            v.wait_ge(sW2, 16 * (Dh + 1))
            ins = None
            for kq in range(11):
                ins = v.scalar_tensor_tensor(wstg[:, Dh % 2, 512 * kq:512 * (kq + 1)],
                                             wstg[:, Dh % 2, 512 * kq:512 * (kq + 1)],
                                             -128.0, sdrep[:, 512 * dq:512 * (dq + 1)],
                                             ALU.add, ALU.mult)
            ins.then_inc(sCV2, 1)

        @blk.vector
        def _(v):
            w = mkser(v)
            v.wait_ge(sROWC, 4)
            w.tensor_copy(drep[:, :], pBig[:, :])
            vsig(v, sD1E)
            v.wait_ge(sROWC, 8)
            w.tensor_copy(sdrep[:, :], pBig[:, :])
            vsig(v, sSDE)
            v.wait_ge(sMXT, 1)
            w.tensor_copy(xsT[0:64, :], pXT[0:64, :, :].rearrange("p c n -> p (c n)"))
            vsig(v, sXT)
            conv_down(v, 0)
            conv_down(v, 1)

        ps3.close()
        psD = ExitStack()
        pDn = psD.enter_context(nc.psum_tensor("pDn", [128, 8, 512], F32))

        @blk.vector
        def _(v):
            w = mkser(v)
            mb = mbuf[:, :, :].rearrange("p a b -> p (a b)")
            for dq in range(4):
                conv_down(v, 4 * dq + 2)
                conv_down(v, 4 * dq + 3)
                if dq < 3:
                    conv_down(v, 4 * dq + 4)
                for tt in range(8):
                    hs = 2 * dq + tt // 4
                    uv = urng[:, hs % 2, :].rearrange("p (t j) -> p t j", j=512)
                    v.wait_ge(sUS, 16 * (hs + 1))
                    w.tensor_mul(mb, uv[:, tt % 4, :],
                                 drep[:, 512 * dq:512 * (dq + 1)])
                    w.tensor_add(uv[:, tt % 4, :], mb, uv[:, tt % 4, :])
                for tt in range(8):
                    hs = 2 * dq + tt // 4
                    uv = urng[:, hs % 2, :].rearrange("p (t j) -> p t j", j=512)
                    v.wait_ge(sDNE, 8 * dq + tt + 1)
                    w.tensor_add(uv[:, tt % 4, :], pDn[:, tt, :], uv[:, tt % 4, :])
                    vsig(v, sEVD)
                if dq < 3:
                    conv_down(v, 4 * dq + 5)

        @blk.sync
        def _(s):
            for h in range(2):
                srcu = u_d[512 * h:512 * (h + 1), 0:512].rearrange(
                    "(t p) j -> p t j", p=128)
                s.dma_start(out=urng[:, h % 2, :].rearrange("p (t j) -> p t j", j=512),
                            in_=srcu).then_inc(sUS, 16)
            for hs in range(2, 10):
                dq, h = hs // 2, hs % 2
                pdq, ph = (hs - 2) // 2, (hs - 2) % 2
                s.wait_ge(sEVD, 8 * pdq + 4 * (ph + 1))
                dsto = out_d[512 * ph:512 * (ph + 1),
                             512 * pdq:512 * (pdq + 1)].rearrange("(t p) j -> p t j", p=128)
                s.dma_start(out=dsto,
                            in_=urng[:, (hs - 2) % 2, :].rearrange("p (t j) -> p t j", j=512)
                            ).then_inc(sOUT, 16)
                if hs < 8:
                    s.wait_ge(sOUT, 16 * (hs - 1))
                    srcu = u_d[512 * h:512 * (h + 1),
                               512 * dq:512 * (dq + 1)].rearrange("(t p) j -> p t j", p=128)
                    s.dma_start(out=urng[:, hs % 2, :].rearrange("p (t j) -> p t j", j=512),
                                in_=srcu).then_inc(sUS, 16)
            s.wait_ge(sOUT, 16 * 8)

        @blk.tensor
        def _(t):
            t.wait_ge(sSDE, 1)
            t.wait_ge(sXT, 1)
            for dq in range(4):
                for kh in range(NHT):
                    q = kh // 11
                    Dh = 4 * dq + q
                    if kh % 11 == 0:
                        t.wait_ge(sCV2, Dh + 1)
                    kq = kh % 11
                    ins = None
                    for tt in range(8):
                        if kh == 0 and dq >= 1:
                            t.wait_ge(sEVD, 8 * (dq - 1) + tt + 1)
                        ins = t.matmul(pDn[:, tt, :],
                                       gate[:, kh, 128 * tt:128 * (tt + 1)],
                                       wstg[:, Dh % 2, 512 * kq:512 * (kq + 1)],
                                       start=(kh == 0), stop=False)
                    if kh % 11 == 10:
                        ins.then_inc(sWDd, 1)
                for tt in range(8):
                    t.matmul(pDn[:, tt, :], xsT[0:64, 128 * tt:128 * (tt + 1)],
                             cmat[0:64, 512 * dq:512 * (dq + 1)],
                             start=False, stop=True).then_inc(sDNE, 1)

        psD.close()

    ctx.close()
    return nc


_NC_CACHE = None


def kernel(**inputs):
    global _NC_CACHE

    u = np.ascontiguousarray(np.asarray(inputs["L0_attn_u"], np.float32))

    def u8view(a_t):
        a = np.ascontiguousarray(a_t)
        return np.ascontiguousarray(
            a.view(np.uint8).reshape(a.shape[0], a.shape[1], 4)[:, :, 0])

    wgt = u8view(np.asarray(inputs["L0_ffn_wq_gate"]).T)
    wut = u8view(np.asarray(inputs["L0_ffn_wq_up"]).T)
    wdt = u8view(np.asarray(inputs["L0_ffn_wq_down"]).T)
    sg = np.asarray(inputs["L0_ffn_s_gate"], np.float32).reshape(NHT, 128).T
    su = np.asarray(inputs["L0_ffn_s_up"], np.float32).reshape(NHT, 128).T
    sgsu = np.ascontiguousarray(np.stack([sg, su], axis=1))
    sdrow = np.ascontiguousarray(np.asarray(inputs["L0_ffn_s_down"], np.float32).reshape(1, DM))
    drow = np.ascontiguousarray(np.asarray(inputs["L0_attn_D"], np.float32).reshape(1, DM))
    bret = np.ascontiguousarray(np.asarray(inputs["L0_attn_B_re"], np.float32).T)
    bimt = np.ascontiguousarray(np.asarray(inputs["L0_attn_B_im"], np.float32).T)
    crt = np.ascontiguousarray(np.asarray(inputs["L0_attn_C_re"], np.float32).T)
    cit = np.ascontiguousarray(np.asarray(inputs["L0_attn_C_im"], np.float32).T)
    prow = np.zeros((1, 128), np.float32)
    prow[0, 0:32] = np.asarray(inputs["L0_attn_Lambda_unconstrained_re"], np.float32)
    prow[0, 32:64] = np.asarray(inputs["L0_attn_Lambda_im"], np.float32)
    prow[0, 64:96] = np.asarray(inputs["L0_attn_log_Delta"], np.float32)[:32]

    identbf = np.eye(128, dtype=BFnp)
    identf = np.eye(32, dtype=np.float32)
    utri = np.triu(np.ones((128, 128), np.float32))
    e127 = np.zeros((128, 1), np.float32); e127[127, 0] = 1.0
    onesr = np.ones((1, 128), np.float32)
    onesrb = np.ones((1, 128), BFnp)
    m128 = np.full((128, 1), -128.0, np.float32)
    onesone = np.ones((1, 1), np.float32)

    if _NC_CACHE is None:
        _NC_CACHE = build_nc()
    nc = _NC_CACHE

    in_maps = []
    for c in range(8):
        b, seg = c // 4, c % 4
        msel = np.zeros((128, 8), np.float32)
        for j in range(4):
            msel[16 * (4 * b + j), j] = 1.0
            msel[16 * (4 * b + j) + 1, 4 + j] = 1.0
        gsel = np.zeros((1, 4), np.float32)
        gsel[0, seg] = 1.0
        in_maps.append({
            "u": np.ascontiguousarray(u[b, R * seg:R * (seg + 1), :]),
            "wgt": wgt, "wut": wut, "wdt": wdt,
            "sgsu": sgsu, "sdrow": sdrow, "drow": drow,
            "bret": bret, "bimt": bimt, "crt": crt, "cit": cit, "prow": prow,
            "identbf": identbf, "identf": identf, "utri": utri, "e127": e127,
            "onesr": onesr, "onesrb": onesrb, "m128": m128, "onesone": onesone,
            "msel": msel, "gsel": gsel,
        })

    res = run_bass_kernel_spmd(nc, in_maps, core_ids=list(range(8)))
    out = np.empty((2, 4096, DM), np.float32)
    for c in range(8):
        b, seg = c // 4, c % 4
        out[b, R * seg:R * (seg + 1), :] = np.asarray(res.results[c]["out"], np.float32)
    return out


# version 14 (final)
# speedup vs baseline: 1.0410x; 1.0410x over previous
import numpy as np
import ml_dtypes
from contextlib import ExitStack

import concourse.bass as bass
import concourse.mybir as mybir
from concourse.bass_utils import run_bass_kernel_spmd
from concourse.replica_groups import maybe_share_collective_output_space

F32 = mybir.dt.float32
BF16 = mybir.dt.bfloat16
U8 = mybir.dt.uint8
AF = mybir.ActivationFunctionType
ALU = mybir.AluOpType
AX = mybir.AxisListType

BFnp = ml_dtypes.bfloat16

R = 1024
DM = 2048
HID = 5632
T = 128
NCH = 8
HALF = 32
NK = 16
NHT = 44
NG = 22
NDQ = 8
RG = [[0, 1, 2, 3, 4, 5, 6, 7]]
NGRP = 2 * NHT

RO_SCR1 = 0
RO_SCR2 = 32
RO_SP = 64
RO_LAMRE = 96
RO_DELTA = 128
RO_AA = 160
RO_BB = 192
RO_NAA = 224
RO_B8 = 256
RO_B16 = 288
RO_S = 320
RO_C = 352
RO_TT1 = 384
RO_TT2 = 416
RO_EA = 448
RO_EAI = 480
RO_LAM1 = 512
RO_EXT = 576
RO_LAM128 = 1088
RO_LAM1024 = 1152
RO_SQA = 1216
RO_SQB = 1280
RO_T1 = 1344
RO_T2 = 1376
RO_GC = 256
RO_GATH = 0
RW_LAMPOW = 0
RW_K = 512
RW_ACC = 576


def build_nc():
    nc = bass.Bass("TRN2", target_bir_lowering=False,
                   detect_race_conditions=False)

    u_d = nc.dram_tensor("u", [R, DM], F32, kind="ExternalInput").ap()
    wgt_d = nc.dram_tensor("wgt", [DM, HID], U8, kind="ExternalInput").ap()
    wut_d = nc.dram_tensor("wut", [DM, HID], U8, kind="ExternalInput").ap()
    wdt_d = nc.dram_tensor("wdt", [HID, DM], U8, kind="ExternalInput").ap()
    sgsu_d = nc.dram_tensor("sgsu", [128, 2, NHT], F32, kind="ExternalInput").ap()
    sdrow_d = nc.dram_tensor("sdrow", [1, DM], F32, kind="ExternalInput").ap()
    drow_d = nc.dram_tensor("drow", [1, DM], F32, kind="ExternalInput").ap()
    bret_d = nc.dram_tensor("bret", [DM, HALF], F32, kind="ExternalInput").ap()
    bimt_d = nc.dram_tensor("bimt", [DM, HALF], F32, kind="ExternalInput").ap()
    crt_d = nc.dram_tensor("crt", [HALF, DM], F32, kind="ExternalInput").ap()
    cit_d = nc.dram_tensor("cit", [HALF, DM], F32, kind="ExternalInput").ap()
    prow_d = nc.dram_tensor("prow", [1, 128], F32, kind="ExternalInput").ap()
    identbf_d = nc.dram_tensor("identbf", [128, 128], BF16, kind="ExternalInput").ap()
    identf_d = nc.dram_tensor("identf", [HALF, HALF], F32, kind="ExternalInput").ap()
    utri_d = nc.dram_tensor("utri", [128, 128], F32, kind="ExternalInput").ap()
    e127_d = nc.dram_tensor("e127", [128, 1], F32, kind="ExternalInput").ap()
    onesr_d = nc.dram_tensor("onesr", [1, 128], F32, kind="ExternalInput").ap()
    onesrb_d = nc.dram_tensor("onesrb", [1, 128], BF16, kind="ExternalInput").ap()
    m128_d = nc.dram_tensor("m128", [128, 1], F32, kind="ExternalInput").ap()
    onesone_d = nc.dram_tensor("onesone", [1, 1], F32, kind="ExternalInput").ap()
    msel_d = nc.dram_tensor("msel", [128, 8], F32, kind="ExternalInput").ap()
    gsel_d = nc.dram_tensor("gsel", [1, 4], F32, kind="ExternalInput").ap()
    out_d = nc.dram_tensor("out", [R, DM], F32, kind="ExternalOutput").ap()

    aspace = maybe_share_collective_output_space("AllGather", RG, prefer_local=True)
    gcin = nc.dram_tensor("gcin", [16, 128], F32)
    gcout = nc.dram_tensor("gcout", [128, 128], F32, addr_space=aspace)

    ctx = ExitStack()
    sb = lambda n, sh, dt: ctx.enter_context(nc.sbuf_tensor(n, sh, dt))
    sem = lambda n: ctx.enter_context(nc.semaphore(n))

    xnT = sb("xnT", [128, NK, R], BF16)
    gate = sb("gate", [128, NHT, R], BF16)
    wstg = sb("wstg", [128, 2, HID], BF16)
    urng = sb("urng", [128, 2, DM], F32)
    xsn = sb("xsn", [128, 512], F32)
    xsb = sb("xsb", [128, 512], BF16)
    qp8p = sb("qp8p", [128, 512], F32)
    xsT = sb("xsT", [64, R], BF16)
    mbuf = sb("mbuf", [128, 2, 256], F32)
    rows = sb("rows", [1, 1536], F32)
    rows2 = sb("rows2", [1, 640], F32)
    bbarT = sb("bbarT", [128, NK, 64], BF16)
    cmat = sb("cmat", [64, DM], BF16)
    sgsu = sb("sgsu_sb", [128, 2, NHT], F32)
    identbf = sb("identbf_sb", [128, 128], BF16)
    identf = sb("identf_sb", [HALF, HALF], F32)
    utri = sb("utri_sb", [128, 128], F32)
    e127 = sb("e127_sb", [128, 1], F32)
    onesr = sb("onesr_sb", [1, 128], F32)
    onesrb = sb("onesrb_sb", [1, 128], BF16)
    m128c = sb("m128c_sb", [128, 1], F32)
    onesone = sb("onesone_sb", [1, 1], F32)
    msel = sb("msel_sb", [128, 8], F32)
    gsel = sb("gsel_sb", [1, 4], F32)
    prow = sb("prow_sb", [1, 128], F32)
    s1c = sb("s1c", [128, 2], F32)
    s2c = sb("s2c", [128, 2], F32)
    meanc = sb("meanc", [128, 2], F32)
    varc = sb("varc", [128, 2], F32)
    rstdc = sb("rstdc", [128, 2], F32)
    negmrc = sb("negmrc", [128, 2], F32)
    scrc = sb("scrc", [128, 2], F32)
    epsc = sb("epsc", [128, 1], F32)
    seedc = sb("seedc", [HALF, 8], F32)
    gath = sb("gath", [128, 128], F32)
    rowbf = sb("rowbf", [1, 8, 512], BF16)
    ctxAB = ExitStack()
    sbAB = lambda n, sh, dt: ctxAB.enter_context(nc.sbuf_tensor(n, sh, dt))
    xn_tm = sbAB("xn_tm", [128, 2, DM], BF16)
    pqcm = sbAB("pqcm", [HALF, 2, 256], F32)
    zz = sbAB("zz", [128, 512], F32)
    qpq = sbAB("qpq", [128, 64], F32)
    tmpB2 = sbAB("tmpB2", [HALF, 4, 128], F32)
    mcol = sbAB("mcol", [HALF, 2], F32)
    tdbl = sbAB("tdbl", [HALF, 2, 64], F32)

    sAUX = sem("sAUX"); sU = sem("sU"); sSQ = sem("sSQ"); sVE1 = sem("sVE1")
    sAC1 = sem("sAC1"); sRSTD = sem("sRSTD"); sLN = sem("sLN"); sTPp = sem("sTPp")
    sTEU = sem("sTEU"); sPV = sem("sPV"); sPA = sem("sPA")
    sSEEDm = sem("sSEEDm"); sSEEDe = sem("sSEEDe"); sPQ = sem("sPQ")
    sTEB = sem("sTEB"); sBBD = sem("sBBD"); sBB = sem("sBB"); sMMB = sem("sMMB")
    sZ = sem("sZ"); sMMC = sem("sMMC"); sXN = sem("sXN"); sEXT = sem("sEXT")
    sGC = sem("sGC"); sCC = sem("sCC"); sGATH = sem("sGATH")
    sW = sem("sW"); sCV = sem("sCV"); sMG = sem("sMG")
    sEG = sem("sEG"); sCM = sem("sCM")
    sSEL = sem("sSEL"); sKC = sem("sKC"); sFIX = sem("sFIX"); sXSB = sem("sXSB")
    sMXT = sem("sMXT"); sXT = sem("sXT")
    sROWB = sem("sROWB"); sROWC = sem("sROWC")
    sD1E = sem("sD1E"); sSDE = sem("sSDE")
    sW2 = sem("sW2"); sCV2 = sem("sCV2")
    sWDd = sem("sWDd"); sCCD = sem("sCCD"); sVS = sem("sVS")
    sUS = sem("sUS"); sEVD = sem("sEVD"); sOUT = sem("sOUT")
    sDNE = sem("sDNE"); sCRm = sem("sCRm"); sBBR = sem("sBBR")
    vsn = [0]

    def mkser(v):
        class _W:
            def __getattr__(self, name):
                fn = getattr(v, name)

                def call(*a, **k):
                    v.wait_ge(sVS, vsn[0])
                    ins = fn(*a, **k)
                    ins.then_inc(sVS, 1)
                    vsn[0] += 1
                    return ins
                return call
        return _W()

    def vsig(v, sm, n=1):
        v.wait_ge(sVS, vsn[0])
        v.engine_nop().then_inc(sm, n)

    def r_(off, n=64):
        return rows[0:1, off:off + n]

    def r2_(off, n=64):
        return rows2[0:1, off:off + n]

    NAUX = 12

    resctx = ExitStack()
    pF32a = resctx.enter_context(nc.psum_tensor("pF32a", [128, 512], F32))
    pF32b = resctx.enter_context(nc.psum_tensor("pF32b", [128, 512], F32))
    pExt = resctx.enter_context(nc.psum_tensor("pExt", [1, 512], F32))

    psA = ExitStack()

    with nc.Block() as blk:

        @blk.sync
        def _(s):
            auxes = [(identbf, identbf_d), (identf, identf_d), (utri, utri_d),
                     (e127, e127_d), (onesr, onesr_d), (onesrb, onesrb_d),
                     (m128c, m128_d), (onesone, onesone_d), (msel, msel_d),
                     (gsel, gsel_d), (prow, prow_d), (sgsu, sgsu_d)]
            assert len(auxes) == NAUX
            for dst, src in auxes:
                idx = tuple(slice(None) for _ in dst.shape)
                s.dma_start(out=dst[idx], in_=src).then_inc(sAUX, 16)
            for i in range(8):
                if i >= 2:
                    s.wait_ge(sLN, i - 1)
                s.dma_start(out=urng[:, i % 2, :],
                            in_=u_d[128 * i:128 * (i + 1), :]).then_inc(sU, 16)

        @blk.gpsimd
        def _(g):
            for S in range(2):
                src = wgt_d[:, 256 * S:256 * (S + 1)].rearrange("(k p) j -> p k j", p=128)
                dst = wstg[:, S % 2, 0:4096].rearrange("p (k j) -> p k j", k=16)
                g.dma_start(out=dst, in_=src).then_inc(sW, 16)

        @blk.vector
        def _(v):
            w = mkser(v)
            v.wait_ge(sAUX, 16 * NAUX)
            Lu = prow[0:1, 0:32]; Lim = prow[0:1, 32:64]
            w.tensor_scalar_mul(r_(RO_TT1, 32), Lu, -1.0)
            w.tensor_max(r_(RO_SCR1, 32), r_(RO_TT1, 32), Lu)
            w.tensor_scalar_mul(r_(RO_SCR1, 32), r_(RO_SCR1, 32), -1.0)
            vsig(v, sPV)
            v.wait_ge(sPA, 1)
            w.tensor_scalar_add(r_(RO_SCR2, 32), r_(RO_SCR2, 32), 1.0)
            vsig(v, sPV)
            v.wait_ge(sPA, 2)
            w.tensor_scalar_max(r_(RO_SP, 32), Lu, 0.0)
            w.tensor_add(r_(RO_SP, 32), r_(RO_SP, 32), r_(RO_SCR2, 32))
            w.tensor_scalar_add(r_(RO_LAMRE, 32), r_(RO_SP, 32), 0.0101)
            w.tensor_scalar_mul(r_(RO_LAMRE, 32), r_(RO_LAMRE, 32), -1.0)
            vsig(v, sPV)
            v.wait_ge(sPA, 3)
            w.tensor_mul(r_(RO_AA, 32), r_(RO_LAMRE, 32), r_(RO_DELTA, 32))
            w.tensor_mul(r_(RO_BB, 32), Lim, r_(RO_DELTA, 32))
            w.tensor_scalar_mul(r_(RO_NAA, 32), r_(RO_AA, 32), -1.0)
            w.tensor_scalar_mul(r_(RO_B8, 32), r_(RO_BB, 32), 0.125)
            w.tensor_scalar_mul(r_(RO_B16, 32), r_(RO_BB, 32), 0.0625)
            vsig(v, sPV)
            v.wait_ge(sPA, 4)
            w.tensor_mul(r_(RO_TT1, 32), r_(RO_B16, 32), r_(RO_B16, 32))
            w.tensor_scalar_mul(r_(RO_TT1, 32), r_(RO_TT1, 32), -2.0)
            w.tensor_scalar_add(r_(RO_C, 32), r_(RO_TT1, 32), 1.0)
            for _i in range(3):
                w.tensor_mul(r_(RO_TT1, 32), r_(RO_S, 32), r_(RO_C, 32))
                w.tensor_mul(r_(RO_TT2, 32), r_(RO_C, 32), r_(RO_C, 32))
                w.tensor_mul(r_(RO_S, 32), r_(RO_S, 32), r_(RO_S, 32))
                w.tensor_sub(r_(RO_C, 32), r_(RO_TT2, 32), r_(RO_S, 32))
                w.tensor_scalar_mul(r_(RO_S, 32), r_(RO_TT1, 32), 2.0)
            w.tensor_mul(r_(RO_LAM1, 32), r_(RO_EA, 32), r_(RO_C, 32))
            w.tensor_mul(rows[0:1, RO_LAM1 + 32:RO_LAM1 + 64], r_(RO_EA, 32), r_(RO_S, 32))
            w.tensor_mul(r_(RO_SQA, 32), r_(RO_EAI, 32), r_(RO_C, 32))
            w.tensor_mul(rows[0:1, RO_SQA + 32:RO_SQA + 64], r_(RO_EAI, 32), r_(RO_S, 32))
            w.tensor_scalar_mul(rows[0:1, RO_SQA + 32:RO_SQA + 64],
                                rows[0:1, RO_SQA + 32:RO_SQA + 64], -1.0)
            w.tensor_mul(r_(RO_SP, 32), r_(RO_LAMRE, 32), r_(RO_LAMRE, 32))
            w.tensor_mul(r_(RO_TT1, 32), Lim, Lim)
            w.tensor_add(r_(RO_SP, 32), r_(RO_SP, 32), r_(RO_TT1, 32))
            w.reciprocal(r_(RO_DELTA, 32), r_(RO_SP, 32))
            w.tensor_scalar_add(r_(RO_AA, 32), r_(RO_LAM1, 32), -1.0)
            w.tensor_mul(r_(RO_TT1, 32), r_(RO_AA, 32), r_(RO_LAMRE, 32))
            w.tensor_mul(r_(RO_TT2, 32), rows[0:1, RO_LAM1 + 32:RO_LAM1 + 64], Lim)
            w.tensor_add(r_(RO_TT1, 32), r_(RO_TT1, 32), r_(RO_TT2, 32))
            w.tensor_mul(r_(RO_SCR1, 32), r_(RO_TT1, 32), r_(RO_DELTA, 32))
            w.tensor_mul(r_(RO_TT1, 32), rows[0:1, RO_LAM1 + 32:RO_LAM1 + 64], r_(RO_LAMRE, 32))
            w.tensor_mul(r_(RO_TT2, 32), r_(RO_AA, 32), Lim)
            w.tensor_sub(r_(RO_TT1, 32), r_(RO_TT1, 32), r_(RO_TT2, 32))
            w.tensor_mul(r_(RO_SCR2, 32), r_(RO_TT1, 32), r_(RO_DELTA, 32))
            vsig(v, sPV)

            w.memset(epsc[:, 0:1], 0.0)
            w.tensor_scalar_add(epsc[:, 0:1], epsc[:, 0:1], 1e-5)
            for i in range(9):
                if i < 8:
                    sl = i % 2
                    v.wait_ge(sU, 16 * (i + 1))
                    w.reduce_sum(s1c[:, sl:sl + 1], urng[:, sl, :], axis=AX.X)
                    v.wait_ge(sSQ, i + 1)
                    w.tensor_scalar_mul(meanc[:, sl:sl + 1], s1c[:, sl:sl + 1], 1.0 / DM)
                    w.tensor_mul(scrc[:, sl:sl + 1], meanc[:, sl:sl + 1], meanc[:, sl:sl + 1])
                    w.scalar_tensor_tensor(varc[:, sl:sl + 1], s2c[:, sl:sl + 1], 1.0 / DM,
                                           scrc[:, sl:sl + 1], ALU.mult, ALU.subtract)
                    vsig(v, sVE1)
                if i < 8:
                    sl = i % 2
                    v.wait_ge(sAC1, i + 1)
                    w.reciprocal(rstdc[:, sl:sl + 1], scrc[:, sl:sl + 1])
                    w.scalar_tensor_tensor(negmrc[:, sl:sl + 1], meanc[:, sl:sl + 1], -1.0,
                                           rstdc[:, sl:sl + 1], ALU.mult, ALU.mult)
                    vsig(v, sRSTD)
            v.wait_ge(sSEEDm, 1)
            w.tensor_copy(seedc[:, 0:6], pF32a[0:HALF, 0:6])
            vsig(v, sSEEDe)

        @blk.scalar
        def _(a):
            a.wait_ge(sAUX, 16 * NAUX)
            a.wait_ge(sPV, 1)
            a.activation(r_(RO_SCR2, 32), r_(RO_SCR1, 32), AF.Exp).then_inc(sPA, 1)
            a.wait_ge(sPV, 2)
            a.activation(r_(RO_SCR2, 32), r_(RO_SCR2, 32), AF.Ln).then_inc(sPA, 1)
            a.wait_ge(sPV, 3)
            a.activation(r_(RO_DELTA, 32), prow[0:1, 64:96], AF.Exp).then_inc(sPA, 1)
            a.wait_ge(sPV, 4)
            a.activation(r_(RO_EA, 32), r_(RO_AA, 32), AF.Exp)
            a.activation(r_(RO_EAI, 32), r_(RO_NAA, 32), AF.Exp)
            a.activation(r_(RO_S, 32), r_(RO_B8, 32), AF.Sin)
            a.activation(r_(RO_B16, 32), r_(RO_B16, 32), AF.Sin).then_inc(sPA, 1)
            a.wait_ge(sU, 16)
            a.activation(gate[:, 0:2, :].rearrange("p a b -> p (a b)"),
                         urng[:, 0, :], AF.Square,
                         accum_out=s2c[:, 0:1]).then_inc(sSQ, 1)
            for i in range(9):
                if i + 1 < 8:
                    sl2 = (i + 1) % 2
                    a.wait_ge(sU, 16 * (i + 2))
                    if i >= 1:
                        a.wait_ge(sVE1, i)
                    a.activation(gate[:, 0:2, :].rearrange("p a b -> p (a b)"),
                                 urng[:, sl2, :], AF.Square,
                                 accum_out=s2c[:, sl2:sl2 + 1]).then_inc(sSQ, 1)
                if i < 8:
                    sl = i % 2
                    a.wait_ge(sVE1, i + 1)
                    a.activation(scrc[:, sl:sl + 1], varc[:, sl:sl + 1],
                                 AF.Sqrt, bias=epsc[:, 0:1]).then_inc(sAC1, 1)
                    a.wait_ge(sRSTD, i + 1)
                    if i >= 2:
                        a.wait_ge(sTPp, 16 * (i - 1))
                    a.activation(xn_tm[:, sl, :], urng[:, sl, :], AF.Identity,
                                 bias=negmrc[:, sl:sl + 1],
                                 scale=rstdc[:, sl:sl + 1]).then_inc(sLN, 1)
                    a.wait_ge(sLN, i + 1)
                    a.dma_start(out=xnT[:, :, 128 * i:128 * (i + 1)],
                                in_=xn_tm[:, sl, :],
                                transpose=True).then_inc(sTPp, 16)

        @blk.tensor
        def _(t):
            t.wait_ge(sAUX, 16 * NAUX)
            t.wait_ge(sPV, 5)
            seeds = [RO_LAM1, RO_LAM1 + 32, RO_SQA, RO_SQA + 32, RO_SCR1, RO_SCR2]
            ins = None
            for j, off in enumerate(seeds):
                ins = t.matmul(pF32a[0:HALF, j:j + 1], rows[0:1, off:off + 32],
                               onesone[0:1, 0:1], start=True, stop=True)
            ins.then_inc(sSEEDm, 1)

    psA.close()

    psB = ExitStack()
    pGa = psB.enter_context(nc.psum_tensor("pGa", [128, 1024], F32))
    pGb = psB.enter_context(nc.psum_tensor("pGb", [128, 1024], F32))
    pAux = psB.enter_context(nc.psum_tensor("pAux", [128, 512], F32))
    pG = [pGa, pGb]

    def stripe_dma(g, S):
        wsrc = wgt_d if S < NG else wut_d
        gg = S % NG
        src = wsrc[:, 256 * gg:256 * (gg + 1)].rearrange("(k p) j -> p k j", p=128)
        dst = wstg[:, S % 2, 0:4096].rearrange("p (k j) -> p k j", k=16)
        if S >= 2:
            g.wait_ge(sMG, 2 * S - 2)
        g.dma_start(out=dst, in_=src).then_inc(sW, 16)

    def conv_stripe(a, S):
        a.wait_ge(sW, 16 * (S + 1))
        a.activation(wstg[:, S % 2, 0:4096], wstg[:, S % 2, 0:4096], AF.Identity,
                     bias=m128c[:, 0:1]).then_inc(sCV, 1)

    def mm_group1(t, G):
        S = G // 2
        hh = G % 2
        t.wait_ge(sCV, S + 1)
        if G >= 2:
            t.wait_ge(sEG, G - 1)
        ins = None
        for k in range(NK):
            lhs = wstg[:, S % 2, 256 * k + 128 * hh:256 * k + 128 * (hh + 1)]
            for th in range(2):
                ins = t.matmul(pG[G % 2][:, 512 * th:512 * (th + 1)], lhs,
                               xnT[:, k, 512 * th:512 * (th + 1)],
                               start=(k == 0), stop=(k == NK - 1))
        ins.then_inc(sMG, 1)

    with nc.Block() as blk:

        @blk.gpsimd
        def _(g):
            for q in range(8):
                srcd = drow_d if q < 4 else sdrow_d
                qq = q % 4
                g.dma_start(out=rowbf[0:1, q, :],
                            in_=srcd[0:1, 512 * qq:512 * (qq + 1)]).then_inc(sROWB, 16)
            for S in range(2, 22):
                stripe_dma(g, S)
            g.wait_ge(sCC, 256)
            g.collective_compute(
                "AllGather", ALU.bypass, replica_groups=RG,
                ins=[gcin.ap().opt()], outs=[gcout.ap().opt()],
            ).then_inc(sCCD)
            for S in range(22, 2 * NG):
                stripe_dma(g, S)
            for Dh in range(2):
                dq, q = Dh // 4, Dh % 4
                g.wait_ge(sMG, 86 + 2 * Dh)
                src = wdt_d[1408 * q:1408 * (q + 1),
                            512 * dq:512 * (dq + 1)].rearrange("(k p) j -> p k j", p=128)
                dst = wstg[:, Dh % 2, 0:5632].rearrange("p (k j) -> p k j", k=11)
                g.dma_start(out=dst, in_=src).then_inc(sW2, 16)

        @blk.sync
        def _(s):
            s.dma_start(out=xsn[:, :].rearrange("p (k c) -> p k c", c=HALF),
                        in_=bret_d.rearrange("(k p) c -> p k c", p=128)).then_inc(sBBD, 16)
            s.dma_start(out=mbuf[:, :, :].rearrange("p a (q c) -> p (a q) c", c=HALF),
                        in_=bimt_d.rearrange("(k p) c -> p k c", p=128)).then_inc(sBBD, 16)
            for n in range(16):
                cpart, cch = n // 8, n % 8
                srcd = crt_d if cpart == 0 else cit_d
                if n >= 1:
                    s.wait_ge(sCM, n)
                s.dma_start(out=tmpB2[:, 0:2, :].rearrange("p a b -> p (a b)"),
                            in_=srcd[:, 256 * cch:256 * (cch + 1)]).then_inc(sBBD, 16)
            s.wait_ge(sGC, 1)
            for rr in range(8):
                s.dma_start(out=gcin.ap()[2 * rr:2 * rr + 1, :],
                            in_=rows[0:1, RO_GC:RO_GC + 128]).then_inc(sCC, 16)
                s.dma_start(out=gcin.ap()[2 * rr + 1:2 * rr + 2, :],
                            in_=rows[0:1, RO_GC + 128:RO_GC + 256]).then_inc(sCC, 16)
            s.wait_ge(sCCD, 1)
            s.dma_start(out=gath[:, :], in_=gcout.ap()).then_inc(sGATH, 16)

        @blk.vector
        def _(v):
            w = mkser(v)

            def csq(dsto, srco):
                w.tensor_mul(r_(RO_T1, 32), rows[0:1, srco:srco + 32], rows[0:1, srco:srco + 32])
                w.tensor_mul(r_(RO_T2, 32), rows[0:1, srco + 32:srco + 64],
                             rows[0:1, srco + 32:srco + 64])
                w.tensor_mul(rows[0:1, dsto + 32:dsto + 64], rows[0:1, srco:srco + 32],
                             rows[0:1, srco + 32:srco + 64])
                w.tensor_scalar_mul(rows[0:1, dsto + 32:dsto + 64],
                                    rows[0:1, dsto + 32:dsto + 64], 2.0)
                w.tensor_sub(rows[0:1, dsto:dsto + 32], r_(RO_T1, 32), r_(RO_T2, 32))

            def wcmul(do, dT, ao, aT, bo, bT):
                t1 = r_(RO_T1, 32); t2 = r_(RO_T2, 32)
                w.tensor_mul(t1, aT(ao, 32), bT(bo, 32))
                w.tensor_mul(t2, aT(ao + 32, 32), bT(bo + 32, 32))
                w.tensor_sub(dT(do, 32), t1, t2)
                w.tensor_mul(t1, aT(ao, 32), bT(bo + 32, 32))
                w.tensor_mul(t2, aT(ao + 32, 32), bT(bo, 32))
                return w.tensor_add(dT(do + 32, 32), t1, t2)

            cur = RO_LAM1
            for i in range(7):
                dst = RO_SQA if i % 2 == 0 else RO_SQB
                if i == 6:
                    dst = RO_LAM128
                csq(dst, cur)
                cur = dst
            cur = RO_LAM128
            for i in range(3):
                dst = RO_SQA if i % 2 == 0 else RO_SQB
                if i == 2:
                    dst = RO_LAM1024
                csq(dst, cur)
                cur = dst
            w.memset(r2_(RW_LAMPOW), 0.0)
            w.tensor_scalar_add(r2_(RW_LAMPOW, 32), r2_(RW_LAMPOW, 32), 1.0)
            for chp in range(7):
                wcmul(RW_LAMPOW + 64 * (chp + 1), r2_, RW_LAMPOW + 64 * chp, r2_,
                      RO_LAM128, r_)

            for sidx, soff in ((0, 2), (1, 0)):
                re = lambda a, b: pqcm[0:HALF, sidx, a:a + b]
                im = lambda a, b: pqcm[0:HALF, sidx, 128 + a:128 + a + b]
                sre = seedc[:, soff:soff + 1]
                sim = seedc[:, soff + 1:soff + 2]
                w.memset(pqcm[0:HALF, sidx, :], 0.0)
                w.tensor_scalar_add(re(0, 1), re(0, 1), 1.0)
                w.tensor_copy(re(1, 1), sre)
                w.tensor_copy(im(1, 1), sim)
                n = 2
                while n < T:
                    m = min(n, T - n)
                    w.tensor_mul(mcol[:, 0:1], re(n - 1, 1), sre)
                    w.tensor_mul(tdbl[:, 0, 0:1], im(n - 1, 1), sim)
                    w.tensor_sub(mcol[:, 0:1], mcol[:, 0:1], tdbl[:, 0, 0:1])
                    w.tensor_mul(mcol[:, 1:2], re(n - 1, 1), sim)
                    w.tensor_mul(tdbl[:, 0, 0:1], im(n - 1, 1), sre)
                    w.tensor_add(mcol[:, 1:2], mcol[:, 1:2], tdbl[:, 0, 0:1])
                    w.tensor_scalar_mul(tdbl[:, 0, 0:m], re(0, m), mcol[:, 0:1])
                    w.tensor_scalar_mul(tdbl[:, 1, 0:m], im(0, m), mcol[:, 1:2])
                    w.tensor_sub(re(n, m), tdbl[:, 0, 0:m], tdbl[:, 1, 0:m])
                    w.tensor_scalar_mul(tdbl[:, 0, 0:m], re(0, m), mcol[:, 1:2])
                    w.tensor_scalar_mul(tdbl[:, 1, 0:m], im(0, m), mcol[:, 0:1])
                    w.tensor_add(im(n, m), tdbl[:, 0, 0:m], tdbl[:, 1, 0:m])
                    n += n
            vsig(v, sPQ)

            cre = urng[:, 0, 0:512]
            cim = urng[:, 1, 0:512]
            mbf = mbuf[:, :, :].rearrange("p a b -> p (a b)")
            v.wait_ge(sCRm, 1)
            w.tensor_copy(urng[:, 0, 0:32], pAux[:, 0:32])
            w.tensor_copy(urng[:, 1, 0:32], pAux[:, 32:64])
            for half in range(2):
                for wd in (32, 64, 128, 256):
                    w.tensor_copy(urng[:, half, wd:2 * wd], urng[:, half, 0:wd])
            v.wait_ge(sBBD, 32)
            w.tensor_mul(zz[:, :], xsn[:, :], cre)
            w.tensor_mul(qp8p[:, :], mbf, cim)
            w.tensor_sub(zz[:, :], zz[:, :], qp8p[:, :])
            w.tensor_mul(qp8p[:, :], xsn[:, :], cim)
            w.tensor_mul(xsn[:, :], mbf, cre)
            w.tensor_add(qp8p[:, :], qp8p[:, :], xsn[:, :])
            w.tensor_copy(bbarT[:, :, 0:32],
                          zz[:, :].rearrange("p (k c) -> p k c", c=32))
            w.tensor_copy(bbarT[:, :, 32:64],
                          qp8p[:, :].rearrange("p (k c) -> p k c", c=32))
            vsig(v, sBBR)

            for n in range(16):
                cpart, cch = n // 8, n % 8
                sgn = 2.0 if cpart == 0 else -2.0
                v.wait_ge(sBBD, 32 + 16 * (n + 1))
                w.tensor_scalar_mul(cmat[32 * cpart:32 * cpart + 32,
                                         256 * cch:256 * (cch + 1)],
                                    tmpB2[:, 0:2, :].rearrange("p a b -> p (a b)"), sgn)
                vsig(v, sCM)

            for j in range(4):
                v.wait_ge(sTPp, 128 + j + 1)
                src = pF32a[:, 0:32] if j % 2 == 0 else pF32b[:, 0:32]
                if j < 2:
                    w.tensor_copy(qpq[:, 32 * j:32 * j + 32], src)
                else:
                    w.tensor_copy(qp8p[:, 32 * (j - 2):32 * (j - 2) + 32], src)
                vsig(v, sTEB)
            for c in range(1, 8):
                w.tensor_copy(qp8p[:, 64 * c:64 * (c + 1)], qp8p[:, 0:64])

            v.wait_ge(sMMB, 8)
            zzv = zz[:, :].rearrange("p (c n) -> p c n", n=64)
            buv = pF32a[:, :].rearrange("p (c n) -> p c n", n=64)
            ztv = mbuf[:, 0, :].rearrange("p (c n) -> p c n", n=32)
            qre = qpq[:, 0:32]; qim = qpq[:, 32:64]
            for ch in range(NCH):
                w.tensor_mul(zzv[:, ch, 0:32], qre, buv[:, ch, 0:32])
                w.tensor_mul(ztv[:, ch, :], qim, buv[:, ch, 32:64])
                w.tensor_sub(zzv[:, ch, 0:32], zzv[:, ch, 0:32], ztv[:, ch, :])
                w.tensor_mul(zzv[:, ch, 32:64], qre, buv[:, ch, 32:64])
                w.tensor_mul(ztv[:, ch, :], qim, buv[:, ch, 0:32])
                w.tensor_add(zzv[:, ch, 32:64], zzv[:, ch, 32:64], ztv[:, ch, :])
            vsig(v, sZ)

            v.wait_ge(sMMC, 1)
            xsv = xsn[:, :].rearrange("p (c n) -> p c n", n=64)
            cuv = pF32b[:, :].rearrange("p (c n) -> p c n", n=64)
            p8 = qp8p[:, :].rearrange("p (c n) -> p c n", n=64)
            m0 = mbuf[:, 0, :].rearrange("p (c n) -> p c n", n=32)
            w.tensor_mul(xsv[:, :, 0:32], p8[:, :, 0:32], cuv[:, :, 0:32])
            w.tensor_mul(m0, p8[:, :, 32:64], cuv[:, :, 32:64])
            w.tensor_sub(xsv[:, :, 0:32], xsv[:, :, 0:32], m0)
            w.tensor_mul(xsv[:, :, 32:64], p8[:, :, 0:32], cuv[:, :, 32:64])
            w.tensor_mul(m0, p8[:, :, 32:64], cuv[:, :, 0:32])
            w.tensor_add(xsv[:, :, 32:64], xsv[:, :, 32:64], m0)
            vsig(v, sXN)

            v.wait_ge(sEXT, 1)
            w.tensor_copy(rows[0:1, RO_EXT:RO_EXT + 512], pExt[0:1, :])
            for ch in range(1, NCH):
                dsto = RO_EXT + 64 * ch
                srco = RO_EXT + 64 * (ch - 1)
                w.tensor_mul(r_(RO_T1, 32), r_(RO_LAM128, 32), rows[0:1, srco:srco + 32])
                w.tensor_mul(r_(RO_T2, 32), rows[0:1, RO_LAM128 + 32:RO_LAM128 + 64],
                             rows[0:1, srco + 32:srco + 64])
                w.tensor_sub(r_(RO_T1, 32), r_(RO_T1, 32), r_(RO_T2, 32))
                w.tensor_mul(r_(RO_T2, 32), r_(RO_LAM128, 32), rows[0:1, srco + 32:srco + 64])
                w.tensor_add(rows[0:1, dsto:dsto + 32], rows[0:1, dsto:dsto + 32], r_(RO_T1, 32))
                w.tensor_mul(r_(RO_T1, 32), rows[0:1, RO_LAM128 + 32:RO_LAM128 + 64],
                             rows[0:1, srco:srco + 32])
                w.tensor_add(r_(RO_T1, 32), r_(RO_T1, 32), r_(RO_T2, 32))
                w.tensor_add(rows[0:1, dsto + 32:dsto + 64], rows[0:1, dsto + 32:dsto + 64],
                             r_(RO_T1, 32))
            w.memset(rows[0:1, RO_GC:RO_GC + 256], 0.0)
            w.tensor_copy(rows[0:1, RO_GC:RO_GC + 64], r_(RO_LAM1024))
            w.tensor_copy(rows[0:1, RO_GC + 128:RO_GC + 192], r_(RO_EXT + 64 * 7))
            vsig(v, sGC)

            def wcmul2(do, dT, ao, aT, bo, bT):
                t1 = r_(RO_T1, 32); t2 = r_(RO_T2, 32)
                w.tensor_mul(t1, aT(ao, 32), bT(bo, 32))
                yield
                w.tensor_mul(t2, aT(ao + 32, 32), bT(bo + 32, 32))
                yield
                w.tensor_sub(dT(do, 32), t1, t2)
                yield
                w.tensor_mul(t1, aT(ao, 32), bT(bo + 32, 32))
                yield
                w.tensor_mul(t2, aT(ao + 32, 32), bT(bo, 32))
                yield
                w.tensor_add(dT(do + 32, 32), t1, t2)
                yield

            def carry_gen():
                v.wait_ge(sSEL, 1)
                w.tensor_copy(rows[0:1, RO_GATH:RO_GATH + 512].rearrange(
                    "o (j n) -> o j n", n=64), pAux[0:1, :].rearrange(
                    "o (j n) -> o j n", n=64))
                yield
                w.memset(r2_(RW_K), 0.0)
                yield
                w.memset(r2_(RW_ACC), 0.0)
                yield
                for j in range(4):
                    for half in range(2):
                        w.tensor_scalar_mul(r_(RO_T1, 32),
                                            rows2[0:1, RW_K + 32 * half:RW_K + 32 * half + 32],
                                            gsel[0:1, j:j + 1])
                        yield
                        w.tensor_add(rows2[0:1, RW_ACC + 32 * half:RW_ACC + 32 * half + 32],
                                     rows2[0:1, RW_ACC + 32 * half:RW_ACC + 32 * half + 32],
                                     r_(RO_T1, 32))
                        yield
                    if j < 3:
                        aoff = RO_GATH + 64 * j
                        soff = RO_GATH + 256 + 64 * j
                        w.tensor_mul(r_(RO_T1, 32), rows[0:1, aoff:aoff + 32],
                                     r2_(RW_K, 32))
                        yield
                        w.tensor_mul(r_(RO_T2, 32), rows[0:1, aoff + 32:aoff + 64],
                                     rows2[0:1, RW_K + 32:RW_K + 64])
                        yield
                        w.tensor_sub(r_(RO_T1, 32), r_(RO_T1, 32), r_(RO_T2, 32))
                        yield
                        w.tensor_mul(r_(RO_T2, 32), rows[0:1, aoff:aoff + 32],
                                     rows2[0:1, RW_K + 32:RW_K + 64])
                        yield
                        w.tensor_mul(rows2[0:1, RW_K + 32:RW_K + 64],
                                     rows[0:1, aoff + 32:aoff + 64], r2_(RW_K, 32))
                        yield
                        w.tensor_add(rows2[0:1, RW_K + 32:RW_K + 64],
                                     rows2[0:1, RW_K + 32:RW_K + 64], r_(RO_T2, 32))
                        yield
                        w.tensor_add(rows2[0:1, RW_K + 32:RW_K + 64],
                                     rows2[0:1, RW_K + 32:RW_K + 64],
                                     rows[0:1, soff + 32:soff + 64])
                        yield
                        w.tensor_add(r2_(RW_K, 32), r_(RO_T1, 32), rows[0:1, soff:soff + 32])
                        yield
                for ch in range(NCH - 1, 0, -1):
                    dsto = RO_EXT + 64 * ch
                    yield from wcmul2(RO_SQA, r_, RW_LAMPOW + 64 * ch, r2_, RW_ACC, r2_)
                    w.tensor_add(r_(RO_SQA, 32), r_(RO_SQA, 32),
                                 rows[0:1, dsto - 64:dsto - 32])
                    yield
                    w.tensor_add(rows[0:1, RO_SQA + 32:RO_SQA + 64],
                                 rows[0:1, RO_SQA + 32:RO_SQA + 64],
                                 rows[0:1, dsto - 32:dsto])
                    yield
                    yield from wcmul2(dsto, r_, RO_LAM1, r_, RO_SQA, r_)
                yield from wcmul2(RO_EXT, r_, RO_LAM1, r_, RW_ACC, r2_)
                vsig(v, sKC)
                yield
                v.wait_ge(sFIX, 1)
                xsv = xsn[:, :].rearrange("p (c n) -> p c n", n=64)
                xbv = xsb[:, :].rearrange("p (c n) -> p c n", n=64)
                fxv = pAux[:, :].rearrange("p (c n) -> p c n", n=64)
                p8b = qp8p[:, :].rearrange("p (c n) -> p c n", n=64)
                f0 = mbuf[:, 0, :].rearrange("p (c n) -> p c n", n=32)
                f1 = mbuf[:, 1, :].rearrange("p (c n) -> p c n", n=32)
                w.tensor_mul(f0, p8b[:, :, 0:32], fxv[:, :, 0:32])
                yield
                w.tensor_mul(f1, p8b[:, :, 32:64], fxv[:, :, 32:64])
                yield
                w.tensor_sub(f0, f0, f1)
                yield
                w.tensor_add(xbv[:, :, 0:32], xsv[:, :, 0:32], f0)
                yield
                w.tensor_mul(f0, p8b[:, :, 0:32], fxv[:, :, 32:64])
                yield
                w.tensor_mul(f1, p8b[:, :, 32:64], fxv[:, :, 0:32])
                yield
                w.tensor_add(f0, f0, f1)
                yield
                w.tensor_add(xbv[:, :, 32:64], xsv[:, :, 32:64], f0)
                vsig(v, sXSB)
                yield

            cg = carry_gen()
            for G in range(NGRP // 2, NGRP):
                ht = G - NGRP // 2
                v.wait_ge(sMG, G + 1)
                v.scalar_tensor_tensor(gate[:, ht, :], pG[G % 2][:, :],
                                       sgsu[:, 1, ht:ht + 1], gate[:, ht, :],
                                       ALU.mult, ALU.mult).then_inc(sEG, 1)
                if G >= NGRP // 2 + 14:
                    for _ in range(7):
                        next(cg, None)
            for _ in cg:
                pass

        @blk.scalar
        def _(a):
            conv_stripe(a, 0)
            conv_stripe(a, 1)
            for G in range(NGRP // 2):
                a.wait_ge(sMG, G + 1)
                a.activation(gate[:, G, :], pG[G % 2][:, :], AF.Silu,
                             scale=sgsu[:, 0, G:G + 1]).then_inc(sEG, 1)
                if G % 2 == 1 and G // 2 + 2 < 24:
                    conv_stripe(a, G // 2 + 2)
            for S in range(24, 2 * NG):
                conv_stripe(a, S)

        @blk.tensor
        def _(t):
            t.wait_ge(sTPp, 128)
            for G in range(NGRP):
                mm_group1(t, G)
                if G == 2:
                    t.matmul(pAux[:, 0:64], onesr[0:1, 0:128], rows[0:1, 0:64],
                             start=True, stop=True).then_inc(sCRm, 1)
                if G == 56:
                    t.wait_ge(sGATH, 16)
                    ins2 = None
                    for j in range(8):
                        ins2 = t.matmul(pAux[0:1, 64 * j:64 * (j + 1)],
                                        msel[:, j:j + 1], gath[:, 0:64],
                                        start=True, stop=True)
                    ins2.then_inc(sSEL, 1)
                if G == 80:
                    t.wait_ge(sKC, 1)
                    ins2 = None
                    for ch in range(NCH):
                        ins2 = t.matmul(pAux[:, 64 * ch:64 * (ch + 1)], onesr[0:1, :],
                                        rows[0:1, RO_EXT + 64 * ch:RO_EXT + 64 * (ch + 1)],
                                        start=True, stop=True)
                    ins2.then_inc(sFIX, 1)
                if G == 16:
                    t.wait_ge(sPQ, 1)
                    t.wait_ge(sSEEDe, 1)
                    for j in range(4):
                        sidx = 0 if j < 2 else 1
                        ri = j % 2
                        if j >= 2:
                            t.wait_ge(sTEB, j - 1)
                        t.transpose(pF32a[:, 0:32] if j % 2 == 0 else pF32b[:, 0:32],
                                    pqcm[0:HALF, sidx, 128 * ri:128 * ri + 128],
                                    identf[:, :]).then_inc(sTPp, 1)
                if G == 20:
                    t.wait_ge(sTEB, 4)
                    t.wait_ge(sBBR, 1)
                    for i in range(8):
                        ins = None
                        for k in range(NK):
                            ins = t.matmul(pF32a[:, 64 * i:64 * (i + 1)],
                                           xnT[:, k, 128 * i:128 * (i + 1)],
                                           bbarT[:, k, :], start=(k == 0),
                                           stop=(k == NK - 1))
                        ins.then_inc(sMMB, 1)
                if G == 24:
                    t.wait_ge(sZ, 1)
                    ins = None
                    for ch in range(NCH):
                        ins = t.matmul(pF32b[:, 64 * ch:64 * (ch + 1)], utri[:, :],
                                       zz[:, 64 * ch:64 * (ch + 1)], start=True, stop=True)
                    ins.then_inc(sMMC, 1)
                if G == 26:
                    t.wait_ge(sXN, 1)
                    t.matmul(pExt[0:1, :], e127[:, :], xsn[:, :],
                             start=True, stop=True).then_inc(sEXT, 1)

    psB.close()
    resctx.close()
    ctxAB.close()
    sdrep = sb("sdrep", [128, DM], BF16)
    drep = sb("drep", [128, DM], BF16)

    with nc.Block() as blk:
        ps3 = ExitStack()
        pXT = ps3.enter_context(nc.psum_tensor("pXT", [64, 8, 128], BF16))
        pBig = ps3.enter_context(nc.psum_tensor("pBig", [128, 2048], F32))

        @blk.gpsimd
        def _(g):
            for Dh in range(2, 16):
                g.wait_ge(sWDd, Dh - 1)
                dq, q = Dh // 4, Dh % 4
                srcw = wdt_d[1408 * q:1408 * (q + 1),
                             512 * dq:512 * (dq + 1)].rearrange(
                                 "(k p) j -> p k j", p=128)
                dstw = wstg[:, Dh % 2, 0:5632].rearrange("p (k j) -> p k j", k=11)
                g.dma_start(out=dstw, in_=srcw).then_inc(sW2, 16)

        @blk.tensor
        def _(t):
            for q in range(8):
                t.wait_ge(sROWB, 16 * (q + 1))
                if q == 4:
                    t.wait_ge(sD1E, 1)
                t.matmul(pBig[:, 512 * (q % 4):512 * (q % 4 + 1)], onesrb[0:1, :],
                         rowbf[0:1, q, :], start=True, stop=True).then_inc(sROWC, 1)
            t.wait_ge(sXSB, 1)
            ins = None
            for ch in range(NCH):
                ins = t.transpose(pXT[:, ch, :], xsb[:, 64 * ch:64 * (ch + 1)],
                                  identbf[:, :])
            ins.then_inc(sMXT, 1)

        def conv_down(v, Dh):
            dq = Dh // 4
            v.wait_ge(sW2, 16 * (Dh + 1))
            ins = None
            for kq in range(11):
                ins = v.scalar_tensor_tensor(wstg[:, Dh % 2, 512 * kq:512 * (kq + 1)],
                                             wstg[:, Dh % 2, 512 * kq:512 * (kq + 1)],
                                             -128.0, sdrep[:, 512 * dq:512 * (dq + 1)],
                                             ALU.add, ALU.mult)
            ins.then_inc(sCV2, 1)

        @blk.vector
        def _(v):
            w = mkser(v)
            v.wait_ge(sROWC, 4)
            w.tensor_copy(drep[:, :], pBig[:, :])
            vsig(v, sD1E)
            v.wait_ge(sROWC, 8)
            w.tensor_copy(sdrep[:, :], pBig[:, :])
            vsig(v, sSDE)
            v.wait_ge(sMXT, 1)
            w.tensor_copy(xsT[0:64, :], pXT[0:64, :, :].rearrange("p c n -> p (c n)"))
            vsig(v, sXT)
            conv_down(v, 0)
            conv_down(v, 1)

        ps3.close()
        psD = ExitStack()
        pDn = psD.enter_context(nc.psum_tensor("pDn", [128, 8, 512], F32))

        @blk.vector
        def _(v):
            w = mkser(v)
            mb = mbuf[:, :, :].rearrange("p a b -> p (a b)")
            for dq in range(4):
                conv_down(v, 4 * dq + 2)
                conv_down(v, 4 * dq + 3)
                if dq < 3:
                    conv_down(v, 4 * dq + 4)
                for tt in range(8):
                    hs = 2 * dq + tt // 4
                    uv = urng[:, hs % 2, :].rearrange("p (t j) -> p t j", j=512)
                    v.wait_ge(sUS, 16 * (hs + 1))
                    w.tensor_mul(mb, uv[:, tt % 4, :],
                                 drep[:, 512 * dq:512 * (dq + 1)])
                    w.tensor_add(uv[:, tt % 4, :], mb, uv[:, tt % 4, :])
                for tt in range(8):
                    hs = 2 * dq + tt // 4
                    uv = urng[:, hs % 2, :].rearrange("p (t j) -> p t j", j=512)
                    v.wait_ge(sDNE, 8 * dq + tt + 1)
                    w.tensor_add(uv[:, tt % 4, :], pDn[:, tt, :], uv[:, tt % 4, :])
                    vsig(v, sEVD)
                if dq < 3:
                    conv_down(v, 4 * dq + 5)

        @blk.sync
        def _(s):
            for h in range(2):
                srcu = u_d[512 * h:512 * (h + 1), 0:512].rearrange(
                    "(t p) j -> p t j", p=128)
                s.dma_start(out=urng[:, h % 2, :].rearrange("p (t j) -> p t j", j=512),
                            in_=srcu).then_inc(sUS, 16)
            for hs in range(2, 10):
                dq, h = hs // 2, hs % 2
                pdq, ph = (hs - 2) // 2, (hs - 2) % 2
                s.wait_ge(sEVD, 8 * pdq + 4 * (ph + 1))
                dsto = out_d[512 * ph:512 * (ph + 1),
                             512 * pdq:512 * (pdq + 1)].rearrange("(t p) j -> p t j", p=128)
                s.dma_start(out=dsto,
                            in_=urng[:, (hs - 2) % 2, :].rearrange("p (t j) -> p t j", j=512)
                            ).then_inc(sOUT, 16)
                if hs < 8:
                    s.wait_ge(sOUT, 16 * (hs - 1))
                    srcu = u_d[512 * h:512 * (h + 1),
                               512 * dq:512 * (dq + 1)].rearrange("(t p) j -> p t j", p=128)
                    s.dma_start(out=urng[:, hs % 2, :].rearrange("p (t j) -> p t j", j=512),
                                in_=srcu).then_inc(sUS, 16)
            s.wait_ge(sOUT, 16 * 8)

        @blk.tensor
        def _(t):
            t.wait_ge(sSDE, 1)
            t.wait_ge(sXT, 1)
            for dq in range(4):
                for kh in range(NHT):
                    q = kh // 11
                    Dh = 4 * dq + q
                    if kh % 11 == 0:
                        t.wait_ge(sCV2, Dh + 1)
                    kq = kh % 11
                    ins = None
                    for tt in range(8):
                        if kh == 0 and dq >= 1:
                            t.wait_ge(sEVD, 8 * (dq - 1) + tt + 1)
                        ins = t.matmul(pDn[:, tt, :],
                                       gate[:, kh, 128 * tt:128 * (tt + 1)],
                                       wstg[:, Dh % 2, 512 * kq:512 * (kq + 1)],
                                       start=(kh == 0), stop=False)
                    if kh % 11 == 10:
                        ins.then_inc(sWDd, 1)
                for tt in range(8):
                    t.matmul(pDn[:, tt, :], xsT[0:64, 128 * tt:128 * (tt + 1)],
                             cmat[0:64, 512 * dq:512 * (dq + 1)],
                             start=False, stop=True).then_inc(sDNE, 1)

        psD.close()

    ctx.close()
    return nc


_NC_CACHE = None


def kernel(**inputs):
    global _NC_CACHE

    u = np.ascontiguousarray(np.asarray(inputs["L0_attn_u"], np.float32))

    def u8view(a_t):
        a = np.ascontiguousarray(a_t)
        return np.ascontiguousarray(
            a.view(np.uint8).reshape(a.shape[0], a.shape[1], 4)[:, :, 0])

    wgt = u8view(np.asarray(inputs["L0_ffn_wq_gate"]).T)
    wut = u8view(np.asarray(inputs["L0_ffn_wq_up"]).T)
    wdt = u8view(np.asarray(inputs["L0_ffn_wq_down"]).T)
    sg = np.asarray(inputs["L0_ffn_s_gate"], np.float32).reshape(NHT, 128).T
    su = np.asarray(inputs["L0_ffn_s_up"], np.float32).reshape(NHT, 128).T
    sgsu = np.ascontiguousarray(np.stack([sg, su], axis=1))
    sdrow = np.ascontiguousarray(np.asarray(inputs["L0_ffn_s_down"], np.float32).reshape(1, DM))
    drow = np.ascontiguousarray(np.asarray(inputs["L0_attn_D"], np.float32).reshape(1, DM))
    bret = np.ascontiguousarray(np.asarray(inputs["L0_attn_B_re"], np.float32).T)
    bimt = np.ascontiguousarray(np.asarray(inputs["L0_attn_B_im"], np.float32).T)
    crt = np.ascontiguousarray(np.asarray(inputs["L0_attn_C_re"], np.float32).T)
    cit = np.ascontiguousarray(np.asarray(inputs["L0_attn_C_im"], np.float32).T)
    prow = np.zeros((1, 128), np.float32)
    prow[0, 0:32] = np.asarray(inputs["L0_attn_Lambda_unconstrained_re"], np.float32)
    prow[0, 32:64] = np.asarray(inputs["L0_attn_Lambda_im"], np.float32)
    prow[0, 64:96] = np.asarray(inputs["L0_attn_log_Delta"], np.float32)[:32]

    identbf = np.eye(128, dtype=BFnp)
    identf = np.eye(32, dtype=np.float32)
    utri = np.triu(np.ones((128, 128), np.float32))
    e127 = np.zeros((128, 1), np.float32); e127[127, 0] = 1.0
    onesr = np.ones((1, 128), np.float32)
    onesrb = np.ones((1, 128), BFnp)
    m128 = np.full((128, 1), -128.0, np.float32)
    onesone = np.ones((1, 1), np.float32)

    if _NC_CACHE is None:
        _NC_CACHE = build_nc()
    nc = _NC_CACHE

    in_maps = []
    for c in range(8):
        b, seg = c // 4, c % 4
        msel = np.zeros((128, 8), np.float32)
        for j in range(4):
            msel[16 * (4 * b + j), j] = 1.0
            msel[16 * (4 * b + j) + 1, 4 + j] = 1.0
        gsel = np.zeros((1, 4), np.float32)
        gsel[0, seg] = 1.0
        in_maps.append({
            "u": np.ascontiguousarray(u[b, R * seg:R * (seg + 1), :]),
            "wgt": wgt, "wut": wut, "wdt": wdt,
            "sgsu": sgsu, "sdrow": sdrow, "drow": drow,
            "bret": bret, "bimt": bimt, "crt": crt, "cit": cit, "prow": prow,
            "identbf": identbf, "identf": identf, "utri": utri, "e127": e127,
            "onesr": onesr, "onesrb": onesrb, "m128": m128, "onesone": onesone,
            "msel": msel, "gsel": gsel,
        })

    res = run_bass_kernel_spmd(nc, in_maps, core_ids=list(range(8)))
    out = np.empty((2, 4096, DM), np.float32)
    for c in range(8):
        b, seg = c // 4, c % 4
        out[b, R * seg:R * (seg + 1), :] = np.asarray(res.results[c]["out"], np.float32)
    return out
